# version 03
# baseline (speedup 1.0000x reference)
import numpy as np
import ml_dtypes
from contextlib import ExitStack

import concourse.bass as bass
import concourse.mybir as mybir
from concourse.bass_utils import run_bass_kernel_spmd

F32 = mybir.dt.float32
BF16 = mybir.dt.bfloat16
I32 = mybir.dt.int32
AF = mybir.ActivationFunctionType
ALU = mybir.AluOpType

B, S, D = 4, 4096, 2048
NS = 16
R = 64
E = 16
I = 1024
CH = 512
NCH = S // CH
DB = D // 128
TT = CH // 128
TOK = S // 2
HPT = 1024
EPS = 1e-5

ENGS = ("sync", "gpsimd", "vector", "scalar", "tensor")
XLN = False
XEV = "scalar"
XBAR = True
XBENGS = ENGS
XDUM = True


class _SkipPhaseB(Exception):
    pass


class Prog:

    def __init__(self):
        self.q = {e: [] for e in ENGS}
        self.cnt = {}

    def sem(self, name):
        self.cnt.setdefault(name, 0)
        return name

    def op(self, eng, fn, inc=None, k=1):
        if inc is not None:
            self.cnt[inc] += k
        self.q[eng].append(["op", fn, inc, k, self.cnt[inc] if inc else 0])

    def wait(self, eng, sem, val=None):
        if val is None:
            val = self.cnt[sem]
        if val > 0:
            self.q[eng].append(("wait", sem, val))

    def val(self, sem):
        return self.cnt[sem]

    def replay(self, eng_name, handle, sems, start, end):
        for item in self.q[eng_name][start:end]:
            if item[0] == "wait":
                handle.wait_ge(sems[item[1]], item[2])
            else:
                _, fn, inc, k, _pv = item
                instr = fn(handle)
                if inc is not None:
                    instr.then_inc(sems[inc], k)

    def marks(self):
        return {e: len(self.q[e]) for e in ENGS}

    def mark_last(self, eng, own_sem):
        for i in range(len(self.q[eng]) - 1, -1, -1):
            it = self.q[eng][i]
            if it[0] == "op":
                if it[2] is None:
                    self.cnt[own_sem] += 1
                    it[2] = own_sem
                    it[3] = 1
                    it[4] = self.cnt[own_sem]
                return (it[2], it[4])
        return None

    def truncate(self, n_ops):
        cnt = {s: 0 for s in self.cnt}
        ptr = {e: 0 for e in ENGS}
        done = 0
        while done < n_ops:
            prog = False
            for e in ENGS:
                while ptr[e] < len(self.q[e]) and done < n_ops:
                    it = self.q[e][ptr[e]]
                    if it[0] == "wait":
                        if cnt[it[1]] < it[2]:
                            break
                    else:
                        if it[2] is not None:
                            cnt[it[2]] += it[3]
                        done += 1
                    ptr[e] += 1
                    prog = True
            if not prog:
                break
        for e in ENGS:
            self.q[e] = self.q[e][:ptr[e]]
        return done


def build_program(dbg=False, n_chunks=NCH, phase_b=True, stage=99, trunc=None):
    nc = bass.Bass("TRN2", target_bir_lowering=False)

    x0 = nc.dram_tensor("x0", [S, D], F32, kind="ExternalInput")
    wx2 = nc.dram_tensor("wx2", [128, DB * 96], F32, kind="ExternalInput")
    wdt = nc.dram_tensor("wdt", [R, D], F32, kind="ExternalInput")
    bdt2 = nc.dram_tensor("bdt2", [128, DB], F32, kind="ExternalInput")
    dp2 = nc.dram_tensor("dp2", [128, DB], F32, kind="ExternalInput")
    alog2 = nc.dram_tensor("alog2", [128, DB * NS], F32, kind="ExternalInput")
    gw2 = nc.dram_tensor("gw2", [128, DB * E], F32, kind="ExternalInput")
    eb128 = nc.dram_tensor("eb128", [128, E], F32, kind="ExternalInput")
    ones1 = nc.dram_tensor("ones1", [1, 128], F32, kind="ExternalInput")
    ident = nc.dram_tensor("ident", [128, 128], F32, kind="ExternalInput")
    sel16 = nc.dram_tensor("sel16", [NS, NS * 128], BF16, kind="ExternalInput")
    toff = nc.dram_tensor("toff", [1, 1], I32, kind="ExternalInput")
    c128 = nc.dram_tensor("c128", [128, 4], F32, kind="ExternalInput")
    wgT = nc.dram_tensor("wgT", [E, D, I], F32, kind="ExternalInput")
    wuT = nc.dram_tensor("wuT", [E, D, I], F32, kind="ExternalInput")
    wdT = nc.dram_tensor("wdT", [E, I, D], F32, kind="ExternalInput")
    swgT = nc.dram_tensor("swgT", [D, I], F32, kind="ExternalInput")
    swuT = nc.dram_tensor("swuT", [D, I], F32, kind="ExternalInput")
    swdT = nc.dram_tensor("swdT", [I, D], F32, kind="ExternalInput")

    out = nc.dram_tensor("out", [TOK, D], F32, kind="ExternalOutput")
    scr_kind = "ExternalOutput" if dbg else "Internal"
    x1_scr = nc.dram_tensor("x1_scr", [S, D], F32, kind=scr_kind)
    cw_scr = nc.dram_tensor("cw_scr", [S, E], F32, kind=scr_kind)
    v_scr = nc.dram_tensor("v_scr", [DB, 128, S], BF16, kind=scr_kind)
    if dbg:
        y_scr = nc.dram_tensor("y_scr", [DB, 128, S], F32, kind="ExternalOutput")
        u_scr = nc.dram_tensor("u_scr", [DB, 128, S], F32, kind="ExternalOutput")
        dt_scr = nc.dram_tensor("dt_scr", [DB, 128, S], F32, kind="ExternalOutput")
        del_scr = nc.dram_tensor("del_scr", [R, S], F32, kind="ExternalOutput")
        stat_scr = nc.dram_tensor("stat_scr", [128, 32], F32, kind="ExternalOutput")
        rout_scr = nc.dram_tensor("rout_scr", [TOK, D], F32, kind="ExternalOutput")
        dump_scr = nc.dram_tensor("dump_scr", [128, 2048], F32, kind="ExternalOutput")
        bb_scr = nc.dram_tensor("bb_scr", [NS, S], BF16, kind="ExternalOutput")

    p = Prog()
    q_s = p.sem("q_s")
    q_g = p.sem("q_g")
    q_o = p.sem("q_o")
    sv = p.sem("sv")
    sa = p.sem("sa")
    st = p.sem("st")
    sg = p.sem("sg")
    ss = p.sem("ss")
    allsems = [q_s, q_g, q_o, sv, sa, st, sg, ss]

    ESEM = {"sync": ss, "gpsimd": sg, "vector": sv, "scalar": sa, "tensor": st}

    def vws(fn):
        p.op("vector", fn, inc=sv)
        p.wait("vector", sv)

    def aws_(fn):
        p.op("scalar", fn, inc=sa)
        p.wait("scalar", sa)

    def barrier():
        if not XBAR:
            return
        p.wait("sync", q_s)
        p.wait("sync", q_o)
        p.wait("gpsimd", q_g)
        snap = {e: p.mark_last(e, ESEM[e]) for e in ENGS}
        for e in ENGS:
            for o in ENGS:
                if o != e and snap[o] is not None:
                    p.wait(e, ESEM[o], snap[o][1] if snap[o][0] == ESEM[o] else None)                         if False else None
        for e in ENGS:
            for o in ENGS:
                if o != e and snap[o] is not None:
                    p.wait(e, snap[o][0], snap[o][1])

    with ExitStack() as ctx:
        en = ctx.enter_context
        semh = {s: en(nc.semaphore(s)) for s in allsems}
        r_t = en(nc.sync.register("r_t"))
        r_a = en(nc.sync.register("r_a"))
        r_c = en(nc.sync.register("r_c"))

        s_wx = en(nc.sbuf_tensor("s_wx", [128, DB * 96], F32))
        s_wdt = en(nc.sbuf_tensor("s_wdt", [R, D], F32))
        s_bdt = en(nc.sbuf_tensor("s_bdt", [128, DB], F32))
        s_dp = en(nc.sbuf_tensor("s_dp", [128, DB], F32))
        s_nA = en(nc.sbuf_tensor("s_nA", [128, DB * NS], F32))
        s_gw = en(nc.sbuf_tensor("s_gw", [128, DB * E], F32))
        s_eb = en(nc.sbuf_tensor("s_eb", [128, E], F32))
        s_ones = en(nc.sbuf_tensor("s_ones", [1, 128], F32))
        s_id = en(nc.sbuf_tensor("s_id", [128, 128], F32))
        s_sel = en(nc.sbuf_tensor("s_sel", [NS, NS * 128], BF16))
        s_toff = en(nc.sbuf_tensor("s_toff", [1, 1], I32))
        s_c = en(nc.sbuf_tensor("s_c", [128, 4], F32))
        s_hcar = en(nc.sbuf_tensor("s_hcar", [128, DB * NS], F32))

        P = [en(nc.psum_tensor(f"P{i}", [128, 512], F32)) for i in range(8)]

        for name, dst, src in (
            ("wx", s_wx, wx2), ("wdt", s_wdt, wdt), ("bdt", s_bdt, bdt2),
            ("dp", s_dp, dp2), ("nA", s_nA, alog2), ("gw", s_gw, gw2),
            ("eb", s_eb, eb128), ("on", s_ones, ones1), ("id", s_id, ident),
            ("sel", s_sel, sel16), ("tof", s_toff, toff), ("c", s_c, c128),
        ):
            p.op("sync", lambda h, d=dst, s=src: h.dma_start(out=d[:], in_=s.ap()), inc=q_s, k=16)
        p.wait("scalar", q_s)
        p.op("scalar", lambda h: h.activation(s_nA[:], s_nA[:], AF.Exp), inc=sa)
        p.wait("vector", sa)
        p.op("vector", lambda h: h.tensor_scalar_mul(s_nA[:], s_nA[:], -1.0), inc=sv)
        p.op("vector", lambda h: h.memset(s_hcar[:], 0.0), inc=sv)
        p.wait("scalar", sv)

        with ExitStack() as actx:
            an = actx.enter_context
            s_tm = an(nc.sbuf_tensor("s_tm", [128, TT * D], F32))
            s_fa = an(nc.sbuf_tensor("s_fa", [128, DB * CH], F32))
            s_fb = an(nc.sbuf_tensor("s_fb", [128, DB * CH], F32))
            s_fc = an(nc.sbuf_tensor("s_fc", [128, DB * CH], F32))
            s_fd = an(nc.sbuf_tensor("s_fd", [128, DB * CH], F32))
            s_a2 = [an(nc.sbuf_tensor(f"s_a{i}", [128, CH], F32)) for i in range(2)]
            s_b2 = [an(nc.sbuf_tensor(f"s_b{i}", [128, CH], F32)) for i in range(2)]
            s_h = an(nc.sbuf_tensor("s_h", [128, CH], F32))
            s_hc = an(nc.sbuf_tensor("s_hc", [128, CH], F32))
            s_del = an(nc.sbuf_tensor("s_del", [R, CH], F32))
            s_bB = an(nc.sbuf_tensor("s_bB", [NS, CH], BF16))
            s_bC = an(nc.sbuf_tensor("s_bC", [NS, CH], BF16))
            s_vbf = an(nc.sbuf_tensor("s_vbf", [128, 2 * CH], BF16))
            s_stat = an(nc.sbuf_tensor("s_stat", [128, 32], F32))
            s_sg4 = an(nc.sbuf_tensor("s_sg4", [128, TT * E], F32))
            s_rt = an(nc.sbuf_tensor("s_rt", [128, E], F32))
            s_m8 = an(nc.sbuf_tensor("s_m8", [128, 8], F32))
            s_cw = an(nc.sbuf_tensor("s_cw", [128, TT * E], F32))

            def ln_stats(src_buf, src_off, dump_buf, dump_off, ncols):
                for a in range(TT):
                    p.op("scalar", lambda h, a=a: h.activation(
                        dump_buf[:, dump_off + a * ncols: dump_off + (a + 1) * ncols],
                        src_buf[:, src_off + a * ncols: src_off + (a + 1) * ncols],
                        AF.Square), inc=sa)
                    p.op("vector", lambda h, a=a: h.reduce_sum(
                        s_stat[:, a:a + 1],
                        src_buf[:, src_off + a * ncols: src_off + (a + 1) * ncols],
                        mybir.AxisListType.X), inc=sv)
                p.wait("vector", sa)
                for a in range(TT):
                    p.op("vector", lambda h, a=a: h.reduce_sum(
                        s_stat[:, 4 + a:5 + a],
                        dump_buf[:, dump_off + a * ncols: dump_off + (a + 1) * ncols],
                        mybir.AxisListType.X), inc=(sv if a == TT - 1 else None))
                p.wait("vector", sv)
                vws(lambda h: h.tensor_scalar_mul(
                    s_stat[:, 8:12], s_stat[:, 0:4], 1.0 / ncols))
                vws(lambda h: h.tensor_mul(
                    s_stat[:, 12:16], s_stat[:, 8:12], s_stat[:, 8:12]))
                vws(lambda h: h.tensor_scalar_mul(
                    s_stat[:, 16:20], s_stat[:, 4:8], 1.0 / ncols))
                vws(lambda h: h.tensor_sub(
                    s_stat[:, 16:20], s_stat[:, 16:20], s_stat[:, 12:16]))
                p.wait("scalar", sv)
                aws_(lambda h: h.activation(
                    s_stat[:, 24:28], s_stat[:, 16:20], AF.Ln, bias=s_c[:, 0:1]))
                aws_(lambda h: h.activation(
                    s_stat[:, 20:24], s_stat[:, 24:28], AF.Exp, scale=s_c[:, 1:2]))
                p.wait("vector", sa)

            for c in range(n_chunks):
                r = c * CH
                p.op("sync", lambda h, r=r: h.dma_start(
                    out=s_tm[:].rearrange("p (a d) -> p a d", a=TT),
                    in_=x0.ap()[r:r + CH, :].rearrange("(a p) d -> p a d", p=128)),
                    inc=q_s, k=16)
                p.wait("scalar", q_s)
                p.wait("vector", q_s)
                if not XLN:
                    ln_stats(s_tm, 0, s_fd, 0, D)
                    for a in range(TT):
                        p.op("vector", lambda h, a=a: h.tensor_scalar(
                            s_tm[:, a * D:(a + 1) * D], s_tm[:, a * D:(a + 1) * D],
                            s_stat[:, 8 + a:9 + a], s_stat[:, 20 + a:21 + a],
                            ALU.subtract, ALU.mult), inc=sv)
                if stage <= 1:
                    barrier()
                    continue
                if dbg and c == 0:
                    p.wait("sync", sv)
                    p.op("sync", lambda h: h.dma_start(out=stat_scr.ap(), in_=s_stat[:]), inc=q_s, k=16)
                    p.op("sync", lambda h: h.dma_start(out=dump_scr.ap(), in_=s_fd[:, 0:D]), inc=q_s, k=16)
                p.wait("tensor", sv)
                for db in range(DB):
                    if c > 0 or db > 1:
                        p.wait("tensor", sa)
                    for a in range(TT):
                        p.op("tensor", lambda h, db=db, a=a: h.transpose(
                            P[db % 2][:, a * 128:(a + 1) * 128],
                            s_tm[:, a * D + db * 128: a * D + (db + 1) * 128],
                            s_id[:]), inc=(st if a == TT - 1 else None))
                    p.wait(XEV, st)
                    if XEV == "scalar":
                        p.op(XEV, lambda h, db=db: h.copy(
                            s_fa[:, db * CH:(db + 1) * CH], P[db % 2][:]), inc=sa)
                    else:
                        p.op(XEV, lambda h, db=db: h.tensor_copy(
                            s_fa[:, db * CH:(db + 1) * CH], P[db % 2][:]), inc=sa)
                if stage <= 2:
                    barrier()
                    continue
                p.wait("tensor", sa)
                for db in range(DB):
                    p.op("tensor", lambda h, db=db: h.matmul(
                        P[2][0:R, :], s_wx[:, db * 96: db * 96 + R],
                        s_fa[:, db * CH:(db + 1) * CH],
                        start=(db == 0), stop=(db == DB - 1)),
                        inc=(st if db == DB - 1 else None))
                p.wait("scalar", st)
                p.op("scalar", lambda h: h.copy(s_del[:], P[2][0:R, :]), inc=sa)
                for db in range(DB):
                    p.op("tensor", lambda h, db=db: h.matmul(
                        P[3][0:NS, :], s_wx[:, db * 96 + R: db * 96 + R + NS],
                        s_fa[:, db * CH:(db + 1) * CH],
                        start=(db == 0), stop=(db == DB - 1)),
                        inc=(st if db == DB - 1 else None))
                p.wait("scalar", st)
                p.op("scalar", lambda h: h.copy(s_bB[:], P[3][0:NS, :]), inc=sa)
                p.wait("tensor", sa)
                for db in range(DB):
                    p.op("tensor", lambda h, db=db: h.matmul(
                        P[2][0:NS, :], s_wx[:, db * 96 + R + NS: db * 96 + R + 2 * NS],
                        s_fa[:, db * CH:(db + 1) * CH],
                        start=(db == 0), stop=(db == DB - 1)),
                        inc=(st if db == DB - 1 else None))
                p.wait("scalar", st)
                p.op("scalar", lambda h: h.copy(s_bC[:], P[2][0:NS, :]), inc=sa)
                if stage <= 3:
                    barrier()
                    continue
                p.wait("tensor", sa)
                for db in range(DB):
                    p.op("tensor", lambda h, db=db: h.matmul(
                        P[2 + db % 2][:], s_wdt[:, db * 128:(db + 1) * 128],
                        s_del[:], start=True, stop=True), inc=st)
                    p.wait("scalar", st)
                    p.op("scalar", lambda h, db=db: h.activation(
                        s_fb[:, db * CH:(db + 1) * CH], P[2 + db % 2][:],
                        AF.Exp, bias=s_bdt[:, db:db + 1]))
                    p.op("scalar", lambda h, db=db: h.activation(
                        s_fb[:, db * CH:(db + 1) * CH],
                        s_fb[:, db * CH:(db + 1) * CH],
                        AF.Ln, bias=s_c[:, 2:3]), inc=sa)
                    if db < DB - 1:
                        p.wait("tensor", sa)
                if stage <= 4:
                    barrier()
                    continue
                if dbg:
                    p.wait("sync", sa)
                    p.op("sync", lambda h, r=r: h.dma_start(
                        out=u_scr.ap().rearrange("j p s -> p j s")[:, :, r:r + CH],
                        in_=s_fa[:].rearrange("p (j t) -> p j t", j=DB)), inc=q_s, k=16)
                    p.op("sync", lambda h, r=r: h.dma_start(
                        out=dt_scr.ap().rearrange("j p s -> p j s")[:, :, r:r + CH],
                        in_=s_fb[:].rearrange("p (j t) -> p j t", j=DB)), inc=q_s, k=16)
                    p.op("sync", lambda h, r=r: h.dma_start(
                        out=del_scr.ap()[:, r:r + CH], in_=s_del[:]), inc=q_s, k=16)
                    p.op("sync", lambda h, r=r: h.dma_start(
                        out=bb_scr.ap()[:, r:r + CH], in_=s_bB[:]), inc=q_s, k=16)
                p.wait("vector", sa)
                for db in range(DB):
                    p.op("vector", lambda h, db=db: h.tensor_mul(
                        s_fc[:, db * CH:(db + 1) * CH],
                        s_fb[:, db * CH:(db + 1) * CH],
                        s_fa[:, db * CH:(db + 1) * CH]), inc=sv)
                for db in range(DB):
                    for n in range(NS):
                        idx = db * NS + n
                        pb, pc = P[4 + idx % 2], P[6 + idx % 2]
                        sA, sB = s_a2[idx % 2], s_b2[idx % 2]
                        if c > 0 or idx > 1:
                            p.wait("scalar", sv)
                            p.wait("tensor", sv)
                        p.op("scalar", lambda h, db=db, idx=idx, sA=sA: h.activation(
                            sA[:], s_fb[:, db * CH:(db + 1) * CH],
                            AF.Exp, scale=s_nA[:, idx:idx + 1]), inc=sa)
                        p.op("tensor", lambda h, n=n, pb=pb: h.matmul(
                            pb[:], s_sel[:, n * 128:(n + 1) * 128], s_bB[:],
                            start=True, stop=True))
                        p.op("tensor", lambda h, n=n, pc=pc: h.matmul(
                            pc[:], s_sel[:, n * 128:(n + 1) * 128], s_bC[:],
                            start=True, stop=True), inc=st)
                        p.wait("vector", sa)
                        p.wait("vector", st)
                        p.op("vector", lambda h, db=db, sB=sB, pb=pb: h.tensor_mul(
                            sB[:], s_fc[:, db * CH:(db + 1) * CH], pb[:]))
                        p.op("vector", lambda h, sA=sA, sB=sB, idx=idx: h.tensor_tensor_scan(
                            s_h[:], sA[:], sB[:], s_hcar[:, idx:idx + 1],
                            ALU.mult, ALU.add))
                        if n == 0:
                            p.op("vector", lambda h, db=db, pc=pc: h.tensor_mul(
                                s_fd[:, db * CH:(db + 1) * CH], s_h[:], pc[:]), inc=sv)
                        else:
                            p.op("vector", lambda h, pc=pc: h.tensor_mul(
                                s_hc[:], s_h[:], pc[:]))
                            p.op("vector", lambda h, db=db: h.tensor_add(
                                s_fd[:, db * CH:(db + 1) * CH],
                                s_fd[:, db * CH:(db + 1) * CH], s_hc[:]), inc=sv)
                        p.op("vector", lambda h, idx=idx: h.tensor_copy(
                            s_hcar[:, idx:idx + 1], s_h[:, CH - 1:CH]))
                    p.op("vector", lambda h, db=db: h.scalar_tensor_tensor(
                        s_fd[:, db * CH:(db + 1) * CH],
                        s_fa[:, db * CH:(db + 1) * CH], s_dp[:, db:db + 1],
                        s_fd[:, db * CH:(db + 1) * CH],
                        ALU.mult, ALU.add), inc=sv)
                v_y_done = p.val(sv)
                if dbg:
                    p.wait("sync", sv, v_y_done)
                    p.op("sync", lambda h, r=r: h.dma_start(
                        out=y_scr.ap().rearrange("j p s -> p j s")[:, :, r:r + CH],
                        in_=s_fd[:].rearrange("p (j t) -> p j t", j=DB)), inc=q_s, k=16)
                if stage <= 5:
                    barrier()
                    continue
                p.wait("tensor", sv, v_y_done)
                for a in range(TT):
                    for dq in range(4):
                        p.wait("tensor", sa)
                        for j in range(4):
                            db = dq * 4 + j
                            p.op("tensor", lambda h, a=a, dq=dq, j=j, db=db: h.transpose(
                                P[(a * 4 + dq) % 2][:, j * 128:(j + 1) * 128],
                                s_fd[:, db * CH + a * 128: db * CH + (a + 1) * 128],
                                s_id[:]), inc=(st if j == 3 else None))
                        p.wait("scalar", st)
                        p.op("scalar", lambda h, a=a, dq=dq: h.copy(
                            s_tm[:, a * D + dq * 512: a * D + (dq + 1) * 512],
                            P[(a * 4 + dq) % 2][:]), inc=sa)
                p.wait("sync", sa)
                p.op("sync", lambda h, r=r: h.dma_start(
                    out=s_fb[:].rearrange("p (a d) -> p a d", a=TT),
                    in_=x0.ap()[r:r + CH, :].rearrange("(a p) d -> p a d", p=128)),
                    inc=q_s, k=16)
                p.wait("scalar", sa)
                p.wait("vector", sa)
                ln_stats(s_tm, 0, s_fa, 0, D)
                p.wait("vector", q_s)
                for a in range(TT):
                    p.op("vector", lambda h, a=a: h.tensor_scalar(
                        s_fc[:, a * D:(a + 1) * D], s_tm[:, a * D:(a + 1) * D],
                        s_stat[:, 8 + a:9 + a], s_stat[:, 20 + a:21 + a],
                        ALU.subtract, ALU.mult))
                    p.op("vector", lambda h, a=a: h.tensor_add(
                        s_fc[:, a * D:(a + 1) * D], s_fc[:, a * D:(a + 1) * D],
                        s_fb[:, a * D:(a + 1) * D]), inc=sv)
                v_x1 = p.val(sv)
                p.wait("sync", sv, v_x1)
                p.op("sync", lambda h, r=r: h.dma_start(
                    out=x1_scr.ap()[r:r + CH, :].rearrange("(a p) d -> p a d", p=128),
                    in_=s_fc[:, 0:TT * D].rearrange("p (a d) -> p a d", a=TT)), inc=q_s, k=16)
                if stage <= 6:
                    barrier()
                    continue
                p.wait("scalar", sv)
                ln_stats(s_fc, 0, s_fa, 0, D)
                for a in range(TT):
                    p.op("vector", lambda h, a=a: h.tensor_scalar(
                        s_fd[:, a * D:(a + 1) * D], s_fc[:, a * D:(a + 1) * D],
                        s_stat[:, 8 + a:9 + a], s_stat[:, 20 + a:21 + a],
                        ALU.subtract, ALU.mult), inc=sv)
                p.wait("tensor", sv)
                for db in range(DB):
                    p.wait("tensor", sa)
                    p.wait("tensor", sv)
                    for a in range(TT):
                        p.op("tensor", lambda h, db=db, a=a: h.transpose(
                            P[db % 2][:, a * 128:(a + 1) * 128],
                            s_fd[:, a * D + db * 128: a * D + (db + 1) * 128],
                            s_id[:]), inc=(st if a == TT - 1 else None))
                    p.wait("scalar", st)
                    p.op("scalar", lambda h, db=db: h.copy(
                        s_fa[:, db * CH:(db + 1) * CH], P[db % 2][:]), inc=sa)
                    p.wait("vector", sa)
                    if db >= 2:
                        p.wait("vector", q_s)
                    p.op("vector", lambda h, db=db: h.tensor_copy(
                        s_vbf[:, (db % 2) * CH:(db % 2 + 1) * CH],
                        s_fa[:, db * CH:(db + 1) * CH]), inc=sv)
                    if db % 2 == 1:
                        p.wait("sync", sv)
                        p.op("sync", lambda h, db=db, r=r: h.dma_start(
                            out=v_scr.ap().rearrange("j p s -> p j s")[:, db - 1:db + 1, r:r + CH],
                            in_=s_vbf[:].rearrange("p (j t) -> p j t", j=2)),
                            inc=q_s, k=16)
                if stage <= 7:
                    barrier()
                    continue
                p.wait("tensor", sa)
                for a in range(TT):
                    pl = P[2 + a % 2]
                    if a > 1:
                        p.wait("tensor", sa)
                    for db in range(DB):
                        p.op("tensor", lambda h, a=a, db=db, pl=pl: h.matmul(
                            pl[:, 0:E], s_fa[:, db * CH + a * 128: db * CH + (a + 1) * 128],
                            s_gw[:, db * E:(db + 1) * E],
                            start=(db == 0), stop=(db == DB - 1)),
                            inc=(st if db == DB - 1 else None))
                    p.wait("scalar", st)
                    p.op("scalar", lambda h, a=a, pl=pl: h.activation(
                        s_sg4[:, a * E:(a + 1) * E], pl[:, 0:E], AF.Sigmoid), inc=sa)
                p.wait("vector", sa)
                for a in range(TT):
                    vws(lambda h, a=a: h.tensor_add(
                        s_rt[:], s_sg4[:, a * E:(a + 1) * E], s_eb[:]))
                    vws(lambda h: h.max(s_m8[:], s_rt[:]))
                    vws(lambda h: h.tensor_scalar(
                        s_rt[:], s_rt[:], s_m8[:, 3:4], None, ALU.is_ge))
                    vws(lambda h, a=a: h.tensor_mul(
                        s_rt[:], s_rt[:], s_sg4[:, a * E:(a + 1) * E]))
                    vws(lambda h: h.reduce_sum(
                        s_m8[:, 4:5], s_rt[:], mybir.AxisListType.X))
                    vws(lambda h: h.reciprocal(s_m8[:, 5:6], s_m8[:, 4:5]))
                    vws(lambda h, a=a: h.tensor_scalar_mul(
                        s_cw[:, a * E:(a + 1) * E], s_rt[:], s_m8[:, 5:6]))
                p.wait("sync", sv)
                p.op("sync", lambda h, r=r: h.dma_start(
                    out=cw_scr.ap()[r:r + CH, :].rearrange("(a p) e -> p a e", p=128),
                    in_=s_cw[:].rearrange("p (a e) -> p a e", a=TT)), inc=q_s, k=16)
                barrier()

        if not phase_b:
            if XDUM:
                p.op("sync", lambda h: h.dma_start(out=out.ap()[0:1, 0:DB * E], in_=s_gw[0:1, :]), inc=q_o, k=16)
                p.wait("sync", q_o)
            if trunc is not None:
                print("truncated to", p.truncate(trunc), "ops")
            ends = p.marks()
            with nc.Block("main") as blk:
                for eng in ENGS:
                    def body(h, eng=eng):
                        p.replay(eng, h, semh, 0, ends[eng])
                    getattr(blk, eng)(body)
            nc._prog = p
            return nc
        with ExitStack() as bctx:
            bn = bctx.enter_context
            s_v = bn(nc.sbuf_tensor("s_v", [128, DB * HPT], BF16))
            s_rout = bn(nc.sbuf_tensor("s_rout", [128, 8 * D], F32))
            s_act = bn(nc.sbuf_tensor("s_act", [128, 8 * HPT], BF16))
            s_wg2 = [bn(nc.sbuf_tensor(f"s_wg{i}", [128, DB * 128], BF16)) for i in range(2)]
            s_wu2 = [bn(nc.sbuf_tensor(f"s_wu{i}", [128, DB * 128], BF16)) for i in range(2)]
            s_wd = bn(nc.sbuf_tensor("s_wd", [128, 8 * D], BF16))
            s_sgl = [bn(nc.sbuf_tensor(f"s_sgl{i}", [128, 512], F32)) for i in range(2)]
            s_cwb = bn(nc.sbuf_tensor("s_cwb", [128, 8 * E], F32))
            s_x1t = bn(nc.sbuf_tensor("s_x1t", [128, D], F32))
            s_outt = bn(nc.sbuf_tensor("s_outt", [128, D], F32))
            s_stB = bn(nc.sbuf_tensor("s_stB", [128, 8], F32))

            for eng in ENGS:
                p.wait(eng, q_s)
            p.op("sync", lambda h: h.reg_load(r_t, s_toff[0:1, 0:1]))

            for hp in range(2):
                p.op("sync", lambda h, hp=hp: h.reg_mov(r_a, hp * HPT))
                p.op("sync", lambda h: h.reg_add(r_a, r_a, r_t))
                p.op("sync", lambda h: h.dma_start(
                    out=s_v[:],
                    in_=bass.AP(v_scr, r_a, [[S, 128], [128 * S, DB], [1, HPT]])),
                    inc=q_s, k=16)
                p.op("sync", lambda h: h.reg_mul(r_c, r_a, E))
                p.op("sync", lambda h: h.dma_start(
                    out=s_cwb[:],
                    in_=bass.AP(cw_scr, r_c, [[E, 128], [128 * E, 8], [1, E]])),
                    inc=q_s, k=16)
                if hp == 1:
                    p.wait("vector", q_o)
                    p.wait("scalar", q_o)
                p.op("vector", lambda h: h.memset(s_rout[:], 0.0), inc=sv)
                p.wait("tensor", q_g)
                p.wait("tensor", q_s)
                p.wait("tensor", sv)
                p.wait("scalar", q_g)
                p.wait("scalar", q_s)

                for e in range(E + 1):
                    shared = (e == E)
                    wg_src = swgT if shared else wgT
                    wu_src = swuT if shared else wuT
                    wd_src = swdT if shared else wdT
                    ebase = 0 if shared else e * D * I
                    dbase = 0 if shared else e * I * D

                    if e > 0:
                        p.wait("gpsimd", st)
                    for half in range(2):
                        p.op("gpsimd", lambda h, half=half, wd_src=wd_src, dbase=dbase: h.dma_start(
                            out=s_wd[:, half * 4 * D:(half + 1) * 4 * D].rearrange(
                                "p (j d) -> p j d", j=4),
                            in_=bass.AP(wd_src, dbase + half * 4 * 128 * D,
                                        [[D, 128], [128 * D, 4], [1, D]])),
                            inc=q_g, k=16)
                    for ib in range(8):
                        if e > 0 or ib > 1:
                            p.wait("gpsimd", sv)
                        p.op("gpsimd", lambda h, ib=ib, wg_src=wg_src, ebase=ebase: h.dma_start(
                            out=s_wg2[ib % 2][:].rearrange("p (j i) -> p j i", j=DB),
                            in_=bass.AP(wg_src, ebase + ib * 128,
                                        [[I, 128], [128 * I, DB], [1, 128]])),
                            inc=q_g, k=16)
                        p.op("gpsimd", lambda h, ib=ib, wu_src=wu_src, ebase=ebase: h.dma_start(
                            out=s_wu2[ib % 2][:].rearrange("p (j i) -> p j i", j=DB),
                            in_=bass.AP(wu_src, ebase + ib * 128,
                                        [[I, 128], [128 * I, DB], [1, 128]])),
                            inc=q_g, k=16)
                        p.wait("tensor", q_g)
                        if e > 0 or ib > 0:
                            p.wait("tensor", sv)
                        for tch in range(2):
                            for db in range(DB):
                                p.op("tensor", lambda h, ib=ib, tch=tch, db=db: h.matmul(
                                    P[4 + tch][:], s_wg2[ib % 2][:, db * 128:(db + 1) * 128],
                                    s_v[:, db * HPT + tch * 512: db * HPT + (tch + 1) * 512],
                                    start=(db == 0), stop=(db == DB - 1)))
                            for db in range(DB):
                                p.op("tensor", lambda h, ib=ib, tch=tch, db=db: h.matmul(
                                    P[6 + tch][:], s_wu2[ib % 2][:, db * 128:(db + 1) * 128],
                                    s_v[:, db * HPT + tch * 512: db * HPT + (tch + 1) * 512],
                                    start=(db == 0), stop=(db == DB - 1)),
                                    inc=(st if db == DB - 1 else None))
                            p.wait("scalar", st)
                            p.op("scalar", lambda h, tch=tch: h.activation(
                                s_sgl[tch][:], P[4 + tch][:], AF.Silu), inc=sa)
                            p.wait("vector", sa)
                            p.op("vector", lambda h, ib=ib, tch=tch: h.tensor_mul(
                                s_act[:, ib * HPT + tch * 512: ib * HPT + (tch + 1) * 512],
                                P[6 + tch][:], s_sgl[tch][:]), inc=sv)
                    v_act_done = p.val(sv)
                    p.wait("tensor", sv, v_act_done)
                    p.wait("tensor", q_g)
                    for tsub in range(8):
                        for dch in range(4):
                            k = (tsub * 4 + dch) % 2
                            if e > 0 or tsub * 4 + dch > 1:
                                p.wait("tensor", sv)
                            for ib in range(8):
                                p.op("tensor", lambda h, tsub=tsub, dch=dch, ib=ib, k=k: h.matmul(
                                    P[k][:], s_act[:, ib * HPT + tsub * 128:
                                                   ib * HPT + (tsub + 1) * 128],
                                    s_wd[:, ib * D + dch * 512: ib * D + (dch + 1) * 512],
                                    start=(ib == 0), stop=(ib == 7)),
                                    inc=(st if ib == 7 else None))
                            p.wait("vector", st)
                            if shared:
                                p.op("vector", lambda h, tsub=tsub, dch=dch, k=k: h.scalar_tensor_tensor(
                                    s_rout[:, tsub * D + dch * 512: tsub * D + (dch + 1) * 512],
                                    P[k][:], s_c[:, 2:3],
                                    s_rout[:, tsub * D + dch * 512: tsub * D + (dch + 1) * 512],
                                    ALU.mult, ALU.add), inc=sv)
                            else:
                                p.op("vector", lambda h, tsub=tsub, dch=dch, k=k, e=e: h.scalar_tensor_tensor(
                                    s_rout[:, tsub * D + dch * 512: tsub * D + (dch + 1) * 512],
                                    P[k][:], s_cwb[:, tsub * E + e: tsub * E + e + 1],
                                    s_rout[:, tsub * D + dch * 512: tsub * D + (dch + 1) * 512],
                                    ALU.mult, ALU.add), inc=sv)
                if dbg:
                    p.wait("sync", sv)
                    p.op("sync", lambda h, hp=hp: h.dma_start(
                        out=rout_scr.ap()[hp * HPT:(hp + 1) * HPT, :].rearrange(
                            "(a p) d -> p a d", p=128),
                        in_=s_rout[:].rearrange("p (a d) -> p a d", a=8)), inc=q_s, k=16)
                p.wait("scalar", sv)
                for tsub in range(8):
                    if tsub > 0:
                        p.wait("sync", q_o)
                        p.wait("scalar", q_o)
                    p.op("sync", lambda h, hp=hp: h.reg_mov(r_a, hp * HPT))
                    p.op("sync", lambda h: h.reg_add(r_a, r_a, r_t))
                    p.op("sync", lambda h, tsub=tsub: h.reg_add(r_a, r_a, tsub * 128))
                    p.op("sync", lambda h: h.reg_mul(r_a, r_a, D))
                    p.op("sync", lambda h: h.dma_start(
                        out=s_x1t[:],
                        in_=bass.AP(x1_scr, r_a, [[D, 128], [1, D]])), inc=q_s, k=16)
                    p.op("scalar", lambda h, tsub=tsub: h.activation(
                        s_outt[:], s_rout[:, tsub * D:(tsub + 1) * D],
                        AF.Square), inc=sa)
                    p.op("vector", lambda h, tsub=tsub: h.reduce_sum(
                        s_stB[:, 0:1], s_rout[:, tsub * D:(tsub + 1) * D],
                        mybir.AxisListType.X))
                    p.wait("vector", sa)
                    vws(lambda h: h.reduce_sum(
                        s_stB[:, 1:2], s_outt[:], mybir.AxisListType.X))
                    vws(lambda h: h.tensor_scalar_mul(
                        s_stB[:, 2:3], s_stB[:, 0:1], 1.0 / D))
                    vws(lambda h: h.tensor_mul(
                        s_stB[:, 3:4], s_stB[:, 2:3], s_stB[:, 2:3]))
                    vws(lambda h: h.tensor_scalar_mul(
                        s_stB[:, 4:5], s_stB[:, 1:2], 1.0 / D))
                    vws(lambda h: h.tensor_sub(
                        s_stB[:, 4:5], s_stB[:, 4:5], s_stB[:, 3:4]))
                    p.wait("scalar", sv)
                    aws_(lambda h: h.activation(
                        s_stB[:, 5:6], s_stB[:, 4:5], AF.Ln, bias=s_c[:, 0:1]))
                    aws_(lambda h: h.activation(
                        s_stB[:, 6:7], s_stB[:, 5:6], AF.Exp, scale=s_c[:, 1:2]))
                    p.wait("vector", sa)
                    p.wait("vector", q_s)
                    p.op("vector", lambda h, tsub=tsub: h.tensor_scalar(
                        s_outt[:], s_rout[:, tsub * D:(tsub + 1) * D],
                        s_stB[:, 2:3], s_stB[:, 6:7], ALU.subtract, ALU.mult))
                    p.op("vector", lambda h: h.tensor_add(
                        s_outt[:], s_outt[:], s_x1t[:]), inc=sv)
                    p.wait("sync", sv)
                    p.op("sync", lambda h, hp=hp, tsub=tsub: h.dma_start(
                        out=out.ap()[hp * HPT + tsub * 128: hp * HPT + (tsub + 1) * 128, :],
                        in_=s_outt[:]), inc=q_o, k=16)
                    if tsub < 7:
                        p.wait("vector", q_o)
                barrier()
            p.wait("sync", q_o)

            if trunc is not None:
                print("truncated to", p.truncate(trunc), "ops")
            ends = p.marks()
            with nc.Block("main") as blk:
                for eng in ENGS:
                    def body(h, eng=eng):
                        p.replay(eng, h, semh, 0, ends[eng])
                    getattr(blk, eng)(body)

    nc._prog = p
    return nc


def _host_layouts(inputs):
    x = np.asarray(inputs["L0_attn_x"], np.float32)
    Wx = np.asarray(inputs["L0_attn_Wx"], np.float32)
    Wdt = np.asarray(inputs["L0_attn_Wdt"], np.float32)
    bdt = np.asarray(inputs["L0_attn_bdt"], np.float32)
    A_log = np.asarray(inputs["L0_attn_A_log"], np.float32)
    Dp = np.asarray(inputs["L0_attn_Dp"], np.float32)
    gate_w = np.asarray(inputs["L0_ffn_gate_w"], np.float32)
    ebias = np.asarray(inputs["L0_ffn_expert_bias"], np.float32)
    w_gate = np.asarray(inputs["L0_ffn_w_gate"], np.float32)
    w_up = np.asarray(inputs["L0_ffn_w_up"], np.float32)
    w_down = np.asarray(inputs["L0_ffn_w_down"], np.float32)
    sw_gate = np.asarray(inputs["L0_ffn_sw_gate"], np.float32)
    sw_up = np.asarray(inputs["L0_ffn_sw_up"], np.float32)
    sw_down = np.asarray(inputs["L0_ffn_sw_down"], np.float32)

    com = {
        "wx2": np.ascontiguousarray(Wx.reshape(DB, 128, 96).transpose(1, 0, 2).reshape(128, DB * 96)),
        "wdt": np.ascontiguousarray(Wdt),
        "bdt2": np.ascontiguousarray(bdt.reshape(DB, 128).T),
        "dp2": np.ascontiguousarray(Dp.reshape(DB, 128).T),
        "alog2": np.ascontiguousarray(A_log.reshape(DB, 128, NS).transpose(1, 0, 2).reshape(128, DB * NS)),
        "gw2": np.ascontiguousarray(gate_w.T.reshape(DB, 128, E).transpose(1, 0, 2).reshape(128, DB * E)),
        "eb128": np.ascontiguousarray(np.broadcast_to(ebias[None, :], (128, E))),
        "ones1": np.ones((1, 128), np.float32),
        "ident": np.eye(128, dtype=np.float32),
        "c128": np.ascontiguousarray(np.broadcast_to(
            np.array([EPS, -0.5, 1.0, 0.0], np.float32)[None, :], (128, 4))),
        "sel16": np.ascontiguousarray(np.kron(np.eye(NS, dtype=np.float32), np.ones((1, 128), np.float32))).astype(ml_dtypes.bfloat16),
        "wgT": np.ascontiguousarray(w_gate.transpose(0, 2, 1)),
        "wuT": np.ascontiguousarray(w_up.transpose(0, 2, 1)),
        "wdT": np.ascontiguousarray(w_down.transpose(0, 2, 1)),
        "swgT": np.ascontiguousarray(sw_gate.T),
        "swuT": np.ascontiguousarray(sw_up.T),
        "swdT": np.ascontiguousarray(sw_down.T),
    }
    in_maps = []
    for c in range(8):
        b, half = c // 2, c % 2
        m = dict(com)
        m["x0"] = np.ascontiguousarray(x[b])
        m["toff"] = np.array([[half * TOK]], np.int32)
        in_maps.append(m)
    return in_maps


def kernel(**inputs):
    in_maps = _host_layouts(inputs)
    nc = build_program(dbg=False)
    res = run_bass_kernel_spmd(nc, in_maps, core_ids=list(range(8)))
    out = np.empty((B, S, D), np.float32)
    for c in range(8):
        b, half = c // 2, c % 2
        out[b, half * TOK:(half + 1) * TOK, :] = res.results[c]["out"]
    return out


# version 05
# speedup vs baseline: 1.4134x; 1.4134x over previous
import numpy as np
import ml_dtypes
from contextlib import ExitStack

import concourse.bass as bass
import concourse.mybir as mybir
from concourse.bass_utils import run_bass_kernel_spmd

F32 = mybir.dt.float32
BF16 = mybir.dt.bfloat16
I32 = mybir.dt.int32
AF = mybir.ActivationFunctionType
ALU = mybir.AluOpType

B, S, D = 4, 4096, 2048
NS = 16
R = 64
E = 16
I = 1024
CH = 512
NCH = S // CH
DB = D // 128
TT = CH // 128
TOK = S // 2
HPT = 1024
EPS = 1e-5

ENGS = ("sync", "gpsimd", "vector", "scalar", "tensor")
XLN = False
XEV = "scalar"
XBAR = True
XBENGS = ENGS
XDUM = True


class _SkipPhaseB(Exception):
    pass


class Prog:

    def __init__(self):
        self.q = {e: [] for e in ENGS}
        self.cnt = {}

    def sem(self, name):
        self.cnt.setdefault(name, 0)
        return name

    def op(self, eng, fn, inc=None, k=1):
        if inc is not None:
            self.cnt[inc] += k
        self.q[eng].append(["op", fn, inc, k, self.cnt[inc] if inc else 0])

    def wait(self, eng, sem, val=None):
        if val is None:
            val = self.cnt[sem]
        if val > 0:
            self.q[eng].append(("wait", sem, val))

    def val(self, sem):
        return self.cnt[sem]

    def replay(self, eng_name, handle, sems, start, end):
        for item in self.q[eng_name][start:end]:
            if item[0] == "wait":
                handle.wait_ge(sems[item[1]], item[2])
            else:
                _, fn, inc, k, _pv = item
                instr = fn(handle)
                if inc is not None:
                    instr.then_inc(sems[inc], k)

    def marks(self):
        return {e: len(self.q[e]) for e in ENGS}

    def mark_last(self, eng, own_sem):
        for i in range(len(self.q[eng]) - 1, -1, -1):
            it = self.q[eng][i]
            if it[0] == "op":
                if it[2] is None:
                    self.cnt[own_sem] += 1
                    it[2] = own_sem
                    it[3] = 1
                    it[4] = self.cnt[own_sem]
                return (it[2], it[4])
        return None

    def truncate(self, n_ops):
        cnt = {s: 0 for s in self.cnt}
        ptr = {e: 0 for e in ENGS}
        done = 0
        while done < n_ops:
            prog = False
            for e in ENGS:
                while ptr[e] < len(self.q[e]) and done < n_ops:
                    it = self.q[e][ptr[e]]
                    if it[0] == "wait":
                        if cnt[it[1]] < it[2]:
                            break
                    else:
                        if it[2] is not None:
                            cnt[it[2]] += it[3]
                        done += 1
                    ptr[e] += 1
                    prog = True
            if not prog:
                break
        for e in ENGS:
            self.q[e] = self.q[e][:ptr[e]]
        return done


def build_program(dbg=False, n_chunks=NCH, phase_b=True, stage=99, trunc=None):
    nc = bass.Bass("TRN2", target_bir_lowering=False)

    x0 = nc.dram_tensor("x0", [S, D], F32, kind="ExternalInput")
    wx2 = nc.dram_tensor("wx2", [128, DB * 96], F32, kind="ExternalInput")
    wdt = nc.dram_tensor("wdt", [R, D], F32, kind="ExternalInput")
    bdt2 = nc.dram_tensor("bdt2", [128, DB], F32, kind="ExternalInput")
    dp2 = nc.dram_tensor("dp2", [128, DB], F32, kind="ExternalInput")
    alog2 = nc.dram_tensor("alog2", [128, DB * NS], F32, kind="ExternalInput")
    gw2 = nc.dram_tensor("gw2", [128, DB * E], F32, kind="ExternalInput")
    eb128 = nc.dram_tensor("eb128", [128, E], F32, kind="ExternalInput")
    ones1 = nc.dram_tensor("ones1", [1, 128], F32, kind="ExternalInput")
    ident = nc.dram_tensor("ident", [128, 128], F32, kind="ExternalInput")
    sel16 = nc.dram_tensor("sel16", [NS, NS * 128], BF16, kind="ExternalInput")
    toff = nc.dram_tensor("toff", [1, 1], I32, kind="ExternalInput")
    c128 = nc.dram_tensor("c128", [128, 4], F32, kind="ExternalInput")
    wgT = nc.dram_tensor("wgT", [E, D, I], F32, kind="ExternalInput")
    wuT = nc.dram_tensor("wuT", [E, D, I], F32, kind="ExternalInput")
    wdT = nc.dram_tensor("wdT", [E, I, D], F32, kind="ExternalInput")
    swgT = nc.dram_tensor("swgT", [D, I], F32, kind="ExternalInput")
    swuT = nc.dram_tensor("swuT", [D, I], F32, kind="ExternalInput")
    swdT = nc.dram_tensor("swdT", [I, D], F32, kind="ExternalInput")

    out = nc.dram_tensor("out", [TOK, D], F32, kind="ExternalOutput")
    scr_kind = "ExternalOutput" if dbg else "Internal"
    x1_scr = nc.dram_tensor("x1_scr", [S, D], F32, kind=scr_kind)
    cw_scr = nc.dram_tensor("cw_scr", [S, E], F32, kind=scr_kind)
    v_scr = nc.dram_tensor("v_scr", [DB, 128, S], BF16, kind=scr_kind)
    if dbg:
        y_scr = nc.dram_tensor("y_scr", [DB, 128, S], F32, kind="ExternalOutput")
        u_scr = nc.dram_tensor("u_scr", [DB, 128, S], F32, kind="ExternalOutput")
        dt_scr = nc.dram_tensor("dt_scr", [DB, 128, S], F32, kind="ExternalOutput")
        del_scr = nc.dram_tensor("del_scr", [R, S], F32, kind="ExternalOutput")
        stat_scr = nc.dram_tensor("stat_scr", [128, 32], F32, kind="ExternalOutput")
        rout_scr = nc.dram_tensor("rout_scr", [TOK, D], F32, kind="ExternalOutput")
        dump_scr = nc.dram_tensor("dump_scr", [128, 2048], F32, kind="ExternalOutput")
        bb_scr = nc.dram_tensor("bb_scr", [NS, S], BF16, kind="ExternalOutput")

    p = Prog()
    q_s = p.sem("q_s")
    q_g = p.sem("q_g")
    q_o = p.sem("q_o")
    sv = p.sem("sv")
    sa = p.sem("sa")
    st = p.sem("st")
    sg = p.sem("sg")
    ss = p.sem("ss")
    allsems = [q_s, q_g, q_o, sv, sa, st, sg, ss]

    ESEM = {"sync": ss, "gpsimd": sg, "vector": sv, "scalar": sa, "tensor": st}

    def vws(fn):
        p.op("vector", fn, inc=sv)
        p.wait("vector", sv)

    def aws_(fn):
        p.op("scalar", fn, inc=sa)
        p.wait("scalar", sa)

    def barrier():
        if not XBAR:
            return
        p.wait("sync", q_s)
        p.wait("sync", q_o)
        p.wait("gpsimd", q_g)
        snap = {e: p.mark_last(e, ESEM[e]) for e in ENGS}
        for e in ENGS:
            for o in ENGS:
                if o != e and snap[o] is not None:
                    p.wait(e, ESEM[o], snap[o][1] if snap[o][0] == ESEM[o] else None)                         if False else None
        for e in ENGS:
            for o in ENGS:
                if o != e and snap[o] is not None:
                    p.wait(e, snap[o][0], snap[o][1])

    with ExitStack() as ctx:
        en = ctx.enter_context
        semh = {s: en(nc.semaphore(s)) for s in allsems}
        r_t = en(nc.sync.register("r_t"))
        r_a = en(nc.sync.register("r_a"))
        r_c = en(nc.sync.register("r_c"))

        s_wx = en(nc.sbuf_tensor("s_wx", [128, DB * 96], F32))
        s_wdt = en(nc.sbuf_tensor("s_wdt", [R, D], F32))
        s_bdt = en(nc.sbuf_tensor("s_bdt", [128, DB], F32))
        s_dp = en(nc.sbuf_tensor("s_dp", [128, DB], F32))
        s_nA = en(nc.sbuf_tensor("s_nA", [128, DB * NS], F32))
        s_gw = en(nc.sbuf_tensor("s_gw", [128, DB * E], F32))
        s_eb = en(nc.sbuf_tensor("s_eb", [128, E], F32))
        s_ones = en(nc.sbuf_tensor("s_ones", [1, 128], F32))
        s_id = en(nc.sbuf_tensor("s_id", [128, 128], F32))
        s_sel = en(nc.sbuf_tensor("s_sel", [NS, NS * 128], BF16))
        s_toff = en(nc.sbuf_tensor("s_toff", [1, 1], I32))
        s_c = en(nc.sbuf_tensor("s_c", [128, 4], F32))
        s_hcar = en(nc.sbuf_tensor("s_hcar", [128, DB * NS], F32))

        P = [en(nc.psum_tensor(f"P{i}", [128, 512], F32)) for i in range(8)]

        for name, dst, src in (
            ("wx", s_wx, wx2), ("wdt", s_wdt, wdt), ("bdt", s_bdt, bdt2),
            ("dp", s_dp, dp2), ("nA", s_nA, alog2), ("gw", s_gw, gw2),
            ("eb", s_eb, eb128), ("on", s_ones, ones1), ("id", s_id, ident),
            ("sel", s_sel, sel16), ("tof", s_toff, toff), ("c", s_c, c128),
        ):
            p.op("sync", lambda h, d=dst, s=src: h.dma_start(out=d[:], in_=s.ap()), inc=q_s, k=16)
        p.wait("scalar", q_s)
        p.op("scalar", lambda h: h.activation(s_nA[:], s_nA[:], AF.Exp), inc=sa)
        p.wait("vector", sa)
        p.op("vector", lambda h: h.tensor_scalar_mul(s_nA[:], s_nA[:], -1.0), inc=sv)
        p.op("vector", lambda h: h.memset(s_hcar[:], 0.0), inc=sv)
        p.wait("scalar", sv)

        with ExitStack() as actx:
            an = actx.enter_context
            s_tm = an(nc.sbuf_tensor("s_tm", [128, TT * D], F32))
            s_fa = an(nc.sbuf_tensor("s_fa", [128, DB * CH], F32))
            s_fb = an(nc.sbuf_tensor("s_fb", [128, DB * CH], F32))
            s_fc = an(nc.sbuf_tensor("s_fc", [128, DB * CH], F32))
            s_fd = an(nc.sbuf_tensor("s_fd", [128, DB * CH], F32))
            s_a2 = [an(nc.sbuf_tensor(f"s_a{i}", [128, CH], F32)) for i in range(2)]
            s_b2 = [an(nc.sbuf_tensor(f"s_b{i}", [128, CH], F32)) for i in range(2)]
            s_h = an(nc.sbuf_tensor("s_h", [128, CH], F32))
            s_hc2 = [an(nc.sbuf_tensor(f"s_hc{i}", [128, CH], F32)) for i in range(2)]
            s_del = an(nc.sbuf_tensor("s_del", [R, CH], F32))
            s_bB = an(nc.sbuf_tensor("s_bB", [NS, CH], BF16))
            s_bC = an(nc.sbuf_tensor("s_bC", [NS, CH], BF16))
            s_vbf = an(nc.sbuf_tensor("s_vbf", [128, 2 * CH], BF16))
            s_stat = an(nc.sbuf_tensor("s_stat", [128, 32], F32))
            s_sg4 = an(nc.sbuf_tensor("s_sg4", [128, TT * E], F32))
            s_rt = an(nc.sbuf_tensor("s_rt", [128, E], F32))
            s_m8 = an(nc.sbuf_tensor("s_m8", [128, 8], F32))
            s_cw = an(nc.sbuf_tensor("s_cw", [128, TT * E], F32))

            def ln_stats(src_buf, src_off, dump_buf, dump_off, ncols):
                for a in range(TT):
                    p.op("scalar", lambda h, a=a: h.activation(
                        dump_buf[:, dump_off + a * ncols: dump_off + (a + 1) * ncols],
                        src_buf[:, src_off + a * ncols: src_off + (a + 1) * ncols],
                        AF.Square), inc=sa)
                    p.op("vector", lambda h, a=a: h.reduce_sum(
                        s_stat[:, a:a + 1],
                        src_buf[:, src_off + a * ncols: src_off + (a + 1) * ncols],
                        mybir.AxisListType.X), inc=sv)
                p.wait("vector", sa)
                for a in range(TT):
                    p.op("vector", lambda h, a=a: h.reduce_sum(
                        s_stat[:, 4 + a:5 + a],
                        dump_buf[:, dump_off + a * ncols: dump_off + (a + 1) * ncols],
                        mybir.AxisListType.X), inc=(sv if a == TT - 1 else None))
                p.wait("vector", sv)
                vws(lambda h: h.tensor_scalar_mul(
                    s_stat[:, 8:12], s_stat[:, 0:4], 1.0 / ncols))
                vws(lambda h: h.tensor_mul(
                    s_stat[:, 12:16], s_stat[:, 8:12], s_stat[:, 8:12]))
                vws(lambda h: h.tensor_scalar_mul(
                    s_stat[:, 16:20], s_stat[:, 4:8], 1.0 / ncols))
                vws(lambda h: h.tensor_sub(
                    s_stat[:, 16:20], s_stat[:, 16:20], s_stat[:, 12:16]))
                p.wait("scalar", sv)
                aws_(lambda h: h.activation(
                    s_stat[:, 24:28], s_stat[:, 16:20], AF.Ln, bias=s_c[:, 0:1]))
                aws_(lambda h: h.activation(
                    s_stat[:, 20:24], s_stat[:, 24:28], AF.Exp, scale=s_c[:, 1:2]))
                p.wait("vector", sa)

            scan_hist = []
            addv = {}
            for c in range(n_chunks):
                r = c * CH
                p.op("sync", lambda h, r=r: h.dma_start(
                    out=s_tm[:].rearrange("p (a d) -> p a d", a=TT),
                    in_=x0.ap()[r:r + CH, :].rearrange("(a p) d -> p a d", p=128)),
                    inc=q_s, k=16)
                p.wait("scalar", q_s)
                p.wait("vector", q_s)
                if not XLN:
                    ln_stats(s_tm, 0, s_fd, 0, D)
                    for a in range(TT):
                        p.op("vector", lambda h, a=a: h.tensor_scalar(
                            s_tm[:, a * D:(a + 1) * D], s_tm[:, a * D:(a + 1) * D],
                            s_stat[:, 8 + a:9 + a], s_stat[:, 20 + a:21 + a],
                            ALU.subtract, ALU.mult), inc=sv)
                if stage <= 1:
                    barrier()
                    continue
                if dbg and c == 0:
                    p.wait("sync", sv)
                    p.op("sync", lambda h: h.dma_start(out=stat_scr.ap(), in_=s_stat[:]), inc=q_s, k=16)
                    p.op("sync", lambda h: h.dma_start(out=dump_scr.ap(), in_=s_fd[:, 0:D]), inc=q_s, k=16)
                p.wait("tensor", sv)
                for db in range(DB):
                    if c > 0 or db > 1:
                        p.wait("tensor", sa)
                    for a in range(TT):
                        p.op("tensor", lambda h, db=db, a=a: h.transpose(
                            P[db % 2][:, a * 128:(a + 1) * 128],
                            s_tm[:, a * D + db * 128: a * D + (db + 1) * 128],
                            s_id[:]), inc=(st if a == TT - 1 else None))
                    p.wait(XEV, st)
                    if XEV == "scalar":
                        p.op(XEV, lambda h, db=db: h.copy(
                            s_fa[:, db * CH:(db + 1) * CH], P[db % 2][:]), inc=sa)
                    else:
                        p.op(XEV, lambda h, db=db: h.tensor_copy(
                            s_fa[:, db * CH:(db + 1) * CH], P[db % 2][:]), inc=sa)
                if stage <= 2:
                    barrier()
                    continue
                p.wait("tensor", sa)
                for db in range(DB):
                    p.op("tensor", lambda h, db=db: h.matmul(
                        P[2][0:R, :], s_wx[:, db * 96: db * 96 + R],
                        s_fa[:, db * CH:(db + 1) * CH],
                        start=(db == 0), stop=(db == DB - 1)),
                        inc=(st if db == DB - 1 else None))
                p.wait("scalar", st)
                p.op("scalar", lambda h: h.copy(s_del[:], P[2][0:R, :]), inc=sa)
                for db in range(DB):
                    p.op("tensor", lambda h, db=db: h.matmul(
                        P[3][0:NS, :], s_wx[:, db * 96 + R: db * 96 + R + NS],
                        s_fa[:, db * CH:(db + 1) * CH],
                        start=(db == 0), stop=(db == DB - 1)),
                        inc=(st if db == DB - 1 else None))
                p.wait("scalar", st)
                p.op("scalar", lambda h: h.copy(s_bB[:], P[3][0:NS, :]), inc=sa)
                p.wait("tensor", sa)
                for db in range(DB):
                    p.op("tensor", lambda h, db=db: h.matmul(
                        P[2][0:NS, :], s_wx[:, db * 96 + R + NS: db * 96 + R + 2 * NS],
                        s_fa[:, db * CH:(db + 1) * CH],
                        start=(db == 0), stop=(db == DB - 1)),
                        inc=(st if db == DB - 1 else None))
                p.wait("scalar", st)
                p.op("scalar", lambda h: h.copy(s_bC[:], P[2][0:NS, :]), inc=sa)
                if stage <= 3:
                    barrier()
                    continue
                p.wait("tensor", sa)
                for db in range(DB):
                    p.op("tensor", lambda h, db=db: h.matmul(
                        P[2 + db % 2][:], s_wdt[:, db * 128:(db + 1) * 128],
                        s_del[:], start=True, stop=True), inc=st)
                    p.wait("scalar", st)
                    p.op("scalar", lambda h, db=db: h.activation(
                        s_fb[:, db * CH:(db + 1) * CH], P[2 + db % 2][:],
                        AF.Exp, bias=s_bdt[:, db:db + 1]))
                    p.op("scalar", lambda h, db=db: h.activation(
                        s_fb[:, db * CH:(db + 1) * CH],
                        s_fb[:, db * CH:(db + 1) * CH],
                        AF.Ln, bias=s_c[:, 2:3]), inc=sa)
                    if db < DB - 1:
                        p.wait("tensor", sa)
                if stage <= 4:
                    barrier()
                    continue
                if dbg:
                    p.wait("sync", sa)
                    p.op("sync", lambda h, r=r: h.dma_start(
                        out=u_scr.ap().rearrange("j p s -> p j s")[:, :, r:r + CH],
                        in_=s_fa[:].rearrange("p (j t) -> p j t", j=DB)), inc=q_s, k=16)
                    p.op("sync", lambda h, r=r: h.dma_start(
                        out=dt_scr.ap().rearrange("j p s -> p j s")[:, :, r:r + CH],
                        in_=s_fb[:].rearrange("p (j t) -> p j t", j=DB)), inc=q_s, k=16)
                    p.op("sync", lambda h, r=r: h.dma_start(
                        out=del_scr.ap()[:, r:r + CH], in_=s_del[:]), inc=q_s, k=16)
                    p.op("sync", lambda h, r=r: h.dma_start(
                        out=bb_scr.ap()[:, r:r + CH], in_=s_bB[:]), inc=q_s, k=16)
                p.wait("vector", sa)
                for db in range(DB):
                    p.op("vector", lambda h, db=db: h.tensor_mul(
                        s_fc[:, db * CH:(db + 1) * CH],
                        s_fb[:, db * CH:(db + 1) * CH],
                        s_fa[:, db * CH:(db + 1) * CH]), inc=sv)
                for db in range(DB):
                    for n in range(NS):
                        idx = db * NS + n
                        pb, pc = P[4 + idx % 2], P[6 + idx % 2]
                        sA, sB = s_a2[idx % 2], s_b2[idx % 2]
                        gidx = len(scan_hist)
                        if gidx >= 2:
                            p.wait("scalar", sv, scan_hist[gidx - 2])
                            p.wait("tensor", sv, scan_hist[gidx - 2])
                            if addv.get(gidx - 2) is not None:
                                p.wait("vector", sg, addv[gidx - 2])
                        p.op("scalar", lambda h, db=db, idx=idx, sA=sA: h.activation(
                            sA[:], s_fb[:, db * CH:(db + 1) * CH],
                            AF.Exp, scale=s_nA[:, idx:idx + 1]), inc=sa)
                        p.op("tensor", lambda h, n=n, pb=pb: h.matmul(
                            pb[:], s_sel[:, n * 128:(n + 1) * 128], s_bB[:],
                            start=True, stop=True))
                        p.op("tensor", lambda h, n=n, pc=pc: h.matmul(
                            pc[:], s_sel[:, n * 128:(n + 1) * 128], s_bC[:],
                            start=True, stop=True), inc=st)
                        p.wait("vector", sa)
                        p.wait("vector", st)
                        p.op("vector", lambda h, db=db, sB=sB, pb=pb: h.tensor_mul(
                            sB[:], s_fc[:, db * CH:(db + 1) * CH], pb[:]))
                        p.op("vector", lambda h, sA=sA, sB=sB, idx=idx: h.tensor_tensor_scan(
                            s_h[:], sA[:], sB[:], s_hcar[:, idx:idx + 1],
                            ALU.mult, ALU.add))
                        if n == 0:
                            p.op("vector", lambda h, db=db, pc=pc: h.tensor_mul(
                                s_fd[:, db * CH:(db + 1) * CH], s_h[:], pc[:]), inc=sv)
                            addv[gidx] = None
                        else:
                            p.op("vector", lambda h, pc=pc, idx=idx: h.tensor_mul(
                                s_hc2[idx % 2][:], s_h[:], pc[:]), inc=sv)
                            p.wait("gpsimd", sv, p.val(sv))
                            p.op("gpsimd", lambda h, db=db, idx=idx: h.tensor_add(
                                s_fd[:, db * CH:(db + 1) * CH],
                                s_fd[:, db * CH:(db + 1) * CH], s_hc2[idx % 2][:]),
                                inc=sg)
                            addv[gidx] = p.val(sg)
                        p.op("vector", lambda h, idx=idx: h.tensor_copy(
                            s_hcar[:, idx:idx + 1], s_h[:, CH - 1:CH]))
                        scan_hist.append(p.val(sv))
                    p.wait("vector", sg, addv[len(scan_hist) - 1])
                    p.op("vector", lambda h, db=db: h.scalar_tensor_tensor(
                        s_fd[:, db * CH:(db + 1) * CH],
                        s_fa[:, db * CH:(db + 1) * CH], s_dp[:, db:db + 1],
                        s_fd[:, db * CH:(db + 1) * CH],
                        ALU.mult, ALU.add), inc=sv)
                v_y_done = p.val(sv)
                if dbg:
                    p.wait("sync", sv, v_y_done)
                    p.op("sync", lambda h, r=r: h.dma_start(
                        out=y_scr.ap().rearrange("j p s -> p j s")[:, :, r:r + CH],
                        in_=s_fd[:].rearrange("p (j t) -> p j t", j=DB)), inc=q_s, k=16)
                if stage <= 5:
                    barrier()
                    continue
                p.wait("tensor", sv, v_y_done)
                for a in range(TT):
                    for dq in range(4):
                        p.wait("tensor", sa)
                        for j in range(4):
                            db = dq * 4 + j
                            p.op("tensor", lambda h, a=a, dq=dq, j=j, db=db: h.transpose(
                                P[(a * 4 + dq) % 2][:, j * 128:(j + 1) * 128],
                                s_fd[:, db * CH + a * 128: db * CH + (a + 1) * 128],
                                s_id[:]), inc=(st if j == 3 else None))
                        p.wait("scalar", st)
                        p.op("scalar", lambda h, a=a, dq=dq: h.copy(
                            s_tm[:, a * D + dq * 512: a * D + (dq + 1) * 512],
                            P[(a * 4 + dq) % 2][:]), inc=sa)
                p.wait("sync", sa)
                p.op("sync", lambda h, r=r: h.dma_start(
                    out=s_fb[:].rearrange("p (a d) -> p a d", a=TT),
                    in_=x0.ap()[r:r + CH, :].rearrange("(a p) d -> p a d", p=128)),
                    inc=q_s, k=16)
                p.wait("scalar", sa)
                p.wait("vector", sa)
                ln_stats(s_tm, 0, s_fa, 0, D)
                p.wait("vector", q_s)
                for a in range(TT):
                    p.op("vector", lambda h, a=a: h.tensor_scalar(
                        s_fc[:, a * D:(a + 1) * D], s_tm[:, a * D:(a + 1) * D],
                        s_stat[:, 8 + a:9 + a], s_stat[:, 20 + a:21 + a],
                        ALU.subtract, ALU.mult))
                    p.op("vector", lambda h, a=a: h.tensor_add(
                        s_fc[:, a * D:(a + 1) * D], s_fc[:, a * D:(a + 1) * D],
                        s_fb[:, a * D:(a + 1) * D]), inc=sv)
                v_x1 = p.val(sv)
                p.wait("sync", sv, v_x1)
                p.op("sync", lambda h, r=r: h.dma_start(
                    out=x1_scr.ap()[r:r + CH, :].rearrange("(a p) d -> p a d", p=128),
                    in_=s_fc[:, 0:TT * D].rearrange("p (a d) -> p a d", a=TT)), inc=q_s, k=16)
                if stage <= 6:
                    barrier()
                    continue
                p.wait("scalar", sv)
                ln_stats(s_fc, 0, s_fa, 0, D)
                for a in range(TT):
                    p.op("vector", lambda h, a=a: h.tensor_scalar(
                        s_fd[:, a * D:(a + 1) * D], s_fc[:, a * D:(a + 1) * D],
                        s_stat[:, 8 + a:9 + a], s_stat[:, 20 + a:21 + a],
                        ALU.subtract, ALU.mult), inc=sv)
                p.wait("tensor", sv)
                for db in range(DB):
                    p.wait("tensor", sa)
                    p.wait("tensor", sv)
                    for a in range(TT):
                        p.op("tensor", lambda h, db=db, a=a: h.transpose(
                            P[db % 2][:, a * 128:(a + 1) * 128],
                            s_fd[:, a * D + db * 128: a * D + (db + 1) * 128],
                            s_id[:]), inc=(st if a == TT - 1 else None))
                    p.wait("scalar", st)
                    p.op("scalar", lambda h, db=db: h.copy(
                        s_fa[:, db * CH:(db + 1) * CH], P[db % 2][:]), inc=sa)
                    p.wait("vector", sa)
                    if db >= 2:
                        p.wait("vector", q_s)
                    p.op("vector", lambda h, db=db: h.tensor_copy(
                        s_vbf[:, (db % 2) * CH:(db % 2 + 1) * CH],
                        s_fa[:, db * CH:(db + 1) * CH]), inc=sv)
                    if db % 2 == 1:
                        p.wait("sync", sv)
                        p.op("sync", lambda h, db=db, r=r: h.dma_start(
                            out=v_scr.ap().rearrange("j p s -> p j s")[:, db - 1:db + 1, r:r + CH],
                            in_=s_vbf[:].rearrange("p (j t) -> p j t", j=2)),
                            inc=q_s, k=16)
                if stage <= 7:
                    barrier()
                    continue
                p.wait("tensor", sa)
                for a in range(TT):
                    pl = P[2 + a % 2]
                    if a > 1:
                        p.wait("tensor", sa)
                    for db in range(DB):
                        p.op("tensor", lambda h, a=a, db=db, pl=pl: h.matmul(
                            pl[:, 0:E], s_fa[:, db * CH + a * 128: db * CH + (a + 1) * 128],
                            s_gw[:, db * E:(db + 1) * E],
                            start=(db == 0), stop=(db == DB - 1)),
                            inc=(st if db == DB - 1 else None))
                    p.wait("scalar", st)
                    p.op("scalar", lambda h, a=a, pl=pl: h.activation(
                        s_sg4[:, a * E:(a + 1) * E], pl[:, 0:E], AF.Sigmoid), inc=sa)
                p.wait("vector", sa)
                for a in range(TT):
                    vws(lambda h, a=a: h.tensor_add(
                        s_rt[:], s_sg4[:, a * E:(a + 1) * E], s_eb[:]))
                    vws(lambda h: h.max(s_m8[:], s_rt[:]))
                    vws(lambda h: h.tensor_scalar(
                        s_rt[:], s_rt[:], s_m8[:, 3:4], None, ALU.is_ge))
                    vws(lambda h, a=a: h.tensor_mul(
                        s_rt[:], s_rt[:], s_sg4[:, a * E:(a + 1) * E]))
                    vws(lambda h: h.reduce_sum(
                        s_m8[:, 4:5], s_rt[:], mybir.AxisListType.X))
                    vws(lambda h: h.reciprocal(s_m8[:, 5:6], s_m8[:, 4:5]))
                    vws(lambda h, a=a: h.tensor_scalar_mul(
                        s_cw[:, a * E:(a + 1) * E], s_rt[:], s_m8[:, 5:6]))
                p.wait("sync", sv)
                p.op("sync", lambda h, r=r: h.dma_start(
                    out=cw_scr.ap()[r:r + CH, :].rearrange("(a p) e -> p a e", p=128),
                    in_=s_cw[:].rearrange("p (a e) -> p a e", a=TT)), inc=q_s, k=16)
                barrier()

        if not phase_b:
            if XDUM:
                p.op("sync", lambda h: h.dma_start(out=out.ap()[0:1, 0:DB * E], in_=s_gw[0:1, :]), inc=q_o, k=16)
                p.wait("sync", q_o)
            if trunc is not None:
                print("truncated to", p.truncate(trunc), "ops")
            ends = p.marks()
            with nc.Block("main") as blk:
                for eng in ENGS:
                    def body(h, eng=eng):
                        p.replay(eng, h, semh, 0, ends[eng])
                    getattr(blk, eng)(body)
            nc._prog = p
            return nc
        with ExitStack() as bctx:
            bn = bctx.enter_context
            s_v = bn(nc.sbuf_tensor("s_v", [128, DB * HPT], BF16))
            s_rout = bn(nc.sbuf_tensor("s_rout", [128, 8 * D], F32))
            s_act = bn(nc.sbuf_tensor("s_act", [128, 8 * HPT], BF16))
            s_wg2 = [bn(nc.sbuf_tensor(f"s_wg{i}", [128, DB * 128], BF16)) for i in range(2)]
            s_wu2 = [bn(nc.sbuf_tensor(f"s_wu{i}", [128, DB * 128], BF16)) for i in range(2)]
            s_wd = bn(nc.sbuf_tensor("s_wd", [128, 8 * D], BF16))
            s_sgl = [bn(nc.sbuf_tensor(f"s_sgl{i}", [128, 512], F32)) for i in range(2)]
            s_cwb = bn(nc.sbuf_tensor("s_cwb", [128, 8 * E], F32))
            s_x1t = bn(nc.sbuf_tensor("s_x1t", [128, D], F32))
            s_outt = bn(nc.sbuf_tensor("s_outt", [128, D], F32))
            s_stB = bn(nc.sbuf_tensor("s_stB", [128, 8], F32))

            for eng in ENGS:
                p.wait(eng, q_s)
            p.op("sync", lambda h: h.reg_load(r_t, s_toff[0:1, 0:1]))

            gctr = 0
            qg_hist = {}
            mm1_hist = {}
            stt_hist = []
            for hp in range(2):
                p.op("sync", lambda h, hp=hp: h.reg_mov(r_a, hp * HPT))
                p.op("sync", lambda h: h.reg_add(r_a, r_a, r_t))
                p.op("sync", lambda h: h.dma_start(
                    out=s_v[:],
                    in_=bass.AP(v_scr, r_a, [[S, 128], [128 * S, DB], [1, HPT]])),
                    inc=q_s, k=16)
                p.op("sync", lambda h: h.reg_mul(r_c, r_a, E))
                p.op("sync", lambda h: h.dma_start(
                    out=s_cwb[:],
                    in_=bass.AP(cw_scr, r_c, [[E, 128], [128 * E, 8], [1, E]])),
                    inc=q_s, k=16)
                if hp == 1:
                    p.wait("vector", q_o)
                    p.wait("scalar", q_o)
                p.op("vector", lambda h: h.memset(s_rout[:], 0.0), inc=sv)
                p.wait("tensor", q_g)
                p.wait("tensor", q_s)
                p.wait("tensor", sv)
                p.wait("scalar", q_g)
                p.wait("scalar", q_s)

                def emit_load(g):
                    e2, ib2 = (g % ((E + 1) * 8)) // 8, g % 8
                    sh2 = (e2 == E)
                    wg_s = swgT if sh2 else wgT
                    wu_s = swuT if sh2 else wuT
                    eb2 = 0 if sh2 else e2 * D * I
                    p.op("gpsimd", lambda h, ib2=ib2, wg_s=wg_s, eb2=eb2: h.dma_start(
                        out=s_wg2[ib2 % 2][:].rearrange("p (j i) -> p j i", j=DB),
                        in_=bass.AP(wg_s, eb2 + ib2 * 128,
                                    [[I, 128], [128 * I, DB], [1, 128]])),
                        inc=q_g, k=16)
                    p.op("gpsimd", lambda h, ib2=ib2, wu_s=wu_s, eb2=eb2: h.dma_start(
                        out=s_wu2[ib2 % 2][:].rearrange("p (j i) -> p j i", j=DB),
                        in_=bass.AP(wu_s, eb2 + ib2 * 128,
                                    [[I, 128], [128 * I, DB], [1, 128]])),
                        inc=q_g, k=16)
                    qg_hist[g] = p.val(q_g)

                GMAX = 2 * (E + 1) * 8 - 1
                for e in range(E + 1):
                    shared = (e == E)
                    wd_src = swdT if shared else wdT
                    dbase = 0 if shared else e * I * D

                    if gctr == 0:
                        emit_load(0)
                        emit_load(1)
                    p.wait("gpsimd", st)
                    for half in range(2):
                        p.op("gpsimd", lambda h, half=half, wd_src=wd_src, dbase=dbase: h.dma_start(
                            out=s_wd[:, half * 4 * D:(half + 1) * 4 * D].rearrange(
                                "p (j d) -> p j d", j=4),
                            in_=bass.AP(wd_src, dbase + half * 4 * 128 * D,
                                        [[D, 128], [128 * D, 4], [1, D]])),
                            inc=q_g, k=16)
                    for ib in range(8):
                        g = gctr + ib
                        p.wait("tensor", q_g, qg_hist[g])
                        if g > 0:
                            p.wait("tensor", sv)
                        for tch in range(2):
                            for db in range(DB):
                                p.op("tensor", lambda h, ib=ib, tch=tch, db=db: h.matmul(
                                    P[4 + tch][:], s_wg2[ib % 2][:, db * 128:(db + 1) * 128],
                                    s_v[:, db * HPT + tch * 512: db * HPT + (tch + 1) * 512],
                                    start=(db == 0), stop=(db == DB - 1)))
                            for db in range(DB):
                                p.op("tensor", lambda h, ib=ib, tch=tch, db=db: h.matmul(
                                    P[6 + tch][:], s_wu2[ib % 2][:, db * 128:(db + 1) * 128],
                                    s_v[:, db * HPT + tch * 512: db * HPT + (tch + 1) * 512],
                                    start=(db == 0), stop=(db == DB - 1)),
                                    inc=(st if db == DB - 1 else None))
                            p.wait("scalar", st)
                            p.op("scalar", lambda h, tch=tch: h.activation(
                                s_sgl[tch][:], P[4 + tch][:], AF.Silu), inc=sa)
                            p.wait("vector", sa)
                            p.op("vector", lambda h, ib=ib, tch=tch: h.tensor_mul(
                                s_act[:, ib * HPT + tch * 512: ib * HPT + (tch + 1) * 512],
                                P[6 + tch][:], s_sgl[tch][:]), inc=sv)
                        mm1_hist[g] = p.val(st)
                        if g + 2 <= GMAX:
                            p.wait("gpsimd", st, mm1_hist[g])
                            emit_load(g + 2)
                    gctr += 8
                    v_act_done = p.val(sv)
                    p.wait("tensor", sv, v_act_done)
                    p.wait("tensor", q_g)
                    for tsub in range(8):
                        for dch in range(4):
                            k = (tsub * 4 + dch) % 2
                            m2 = len(stt_hist)
                            if m2 >= 2:
                                p.wait("tensor", sv, stt_hist[m2 - 2])
                            for ib in range(8):
                                p.op("tensor", lambda h, tsub=tsub, dch=dch, ib=ib, k=k: h.matmul(
                                    P[k][:], s_act[:, ib * HPT + tsub * 128:
                                                   ib * HPT + (tsub + 1) * 128],
                                    s_wd[:, ib * D + dch * 512: ib * D + (dch + 1) * 512],
                                    start=(ib == 0), stop=(ib == 7)),
                                    inc=(st if ib == 7 else None))
                            p.wait("vector", st)
                            if shared:
                                p.op("vector", lambda h, tsub=tsub, dch=dch, k=k: h.scalar_tensor_tensor(
                                    s_rout[:, tsub * D + dch * 512: tsub * D + (dch + 1) * 512],
                                    P[k][:], s_c[:, 2:3],
                                    s_rout[:, tsub * D + dch * 512: tsub * D + (dch + 1) * 512],
                                    ALU.mult, ALU.add), inc=sv)
                            else:
                                p.op("vector", lambda h, tsub=tsub, dch=dch, k=k, e=e: h.scalar_tensor_tensor(
                                    s_rout[:, tsub * D + dch * 512: tsub * D + (dch + 1) * 512],
                                    P[k][:], s_cwb[:, tsub * E + e: tsub * E + e + 1],
                                    s_rout[:, tsub * D + dch * 512: tsub * D + (dch + 1) * 512],
                                    ALU.mult, ALU.add), inc=sv)
                            stt_hist.append(p.val(sv))
                if dbg:
                    p.wait("sync", sv)
                    p.op("sync", lambda h, hp=hp: h.dma_start(
                        out=rout_scr.ap()[hp * HPT:(hp + 1) * HPT, :].rearrange(
                            "(a p) d -> p a d", p=128),
                        in_=s_rout[:].rearrange("p (a d) -> p a d", a=8)), inc=q_s, k=16)
                p.wait("scalar", sv)
                for tsub in range(8):
                    if tsub > 0:
                        p.wait("sync", q_o)
                        p.wait("scalar", q_o)
                    p.op("sync", lambda h, hp=hp: h.reg_mov(r_a, hp * HPT))
                    p.op("sync", lambda h: h.reg_add(r_a, r_a, r_t))
                    p.op("sync", lambda h, tsub=tsub: h.reg_add(r_a, r_a, tsub * 128))
                    p.op("sync", lambda h: h.reg_mul(r_a, r_a, D))
                    p.op("sync", lambda h: h.dma_start(
                        out=s_x1t[:],
                        in_=bass.AP(x1_scr, r_a, [[D, 128], [1, D]])), inc=q_s, k=16)
                    p.op("scalar", lambda h, tsub=tsub: h.activation(
                        s_outt[:], s_rout[:, tsub * D:(tsub + 1) * D],
                        AF.Square), inc=sa)
                    p.op("vector", lambda h, tsub=tsub: h.reduce_sum(
                        s_stB[:, 0:1], s_rout[:, tsub * D:(tsub + 1) * D],
                        mybir.AxisListType.X))
                    p.wait("vector", sa)
                    vws(lambda h: h.reduce_sum(
                        s_stB[:, 1:2], s_outt[:], mybir.AxisListType.X))
                    vws(lambda h: h.tensor_scalar_mul(
                        s_stB[:, 2:3], s_stB[:, 0:1], 1.0 / D))
                    vws(lambda h: h.tensor_mul(
                        s_stB[:, 3:4], s_stB[:, 2:3], s_stB[:, 2:3]))
                    vws(lambda h: h.tensor_scalar_mul(
                        s_stB[:, 4:5], s_stB[:, 1:2], 1.0 / D))
                    vws(lambda h: h.tensor_sub(
                        s_stB[:, 4:5], s_stB[:, 4:5], s_stB[:, 3:4]))
                    p.wait("scalar", sv)
                    aws_(lambda h: h.activation(
                        s_stB[:, 5:6], s_stB[:, 4:5], AF.Ln, bias=s_c[:, 0:1]))
                    aws_(lambda h: h.activation(
                        s_stB[:, 6:7], s_stB[:, 5:6], AF.Exp, scale=s_c[:, 1:2]))
                    p.wait("vector", sa)
                    p.wait("vector", q_s)
                    p.op("vector", lambda h, tsub=tsub: h.tensor_scalar(
                        s_outt[:], s_rout[:, tsub * D:(tsub + 1) * D],
                        s_stB[:, 2:3], s_stB[:, 6:7], ALU.subtract, ALU.mult))
                    p.op("vector", lambda h: h.tensor_add(
                        s_outt[:], s_outt[:], s_x1t[:]), inc=sv)
                    p.wait("sync", sv)
                    p.op("sync", lambda h, hp=hp, tsub=tsub: h.dma_start(
                        out=out.ap()[hp * HPT + tsub * 128: hp * HPT + (tsub + 1) * 128, :],
                        in_=s_outt[:]), inc=q_o, k=16)
                    if tsub < 7:
                        p.wait("vector", q_o)
                barrier()
            p.wait("sync", q_o)

            if trunc is not None:
                print("truncated to", p.truncate(trunc), "ops")
            ends = p.marks()
            with nc.Block("main") as blk:
                for eng in ENGS:
                    def body(h, eng=eng):
                        p.replay(eng, h, semh, 0, ends[eng])
                    getattr(blk, eng)(body)

    nc._prog = p
    return nc


def _host_layouts(inputs):
    x = np.asarray(inputs["L0_attn_x"], np.float32)
    Wx = np.asarray(inputs["L0_attn_Wx"], np.float32)
    Wdt = np.asarray(inputs["L0_attn_Wdt"], np.float32)
    bdt = np.asarray(inputs["L0_attn_bdt"], np.float32)
    A_log = np.asarray(inputs["L0_attn_A_log"], np.float32)
    Dp = np.asarray(inputs["L0_attn_Dp"], np.float32)
    gate_w = np.asarray(inputs["L0_ffn_gate_w"], np.float32)
    ebias = np.asarray(inputs["L0_ffn_expert_bias"], np.float32)
    w_gate = np.asarray(inputs["L0_ffn_w_gate"], np.float32)
    w_up = np.asarray(inputs["L0_ffn_w_up"], np.float32)
    w_down = np.asarray(inputs["L0_ffn_w_down"], np.float32)
    sw_gate = np.asarray(inputs["L0_ffn_sw_gate"], np.float32)
    sw_up = np.asarray(inputs["L0_ffn_sw_up"], np.float32)
    sw_down = np.asarray(inputs["L0_ffn_sw_down"], np.float32)

    com = {
        "wx2": np.ascontiguousarray(Wx.reshape(DB, 128, 96).transpose(1, 0, 2).reshape(128, DB * 96)),
        "wdt": np.ascontiguousarray(Wdt),
        "bdt2": np.ascontiguousarray(bdt.reshape(DB, 128).T),
        "dp2": np.ascontiguousarray(Dp.reshape(DB, 128).T),
        "alog2": np.ascontiguousarray(A_log.reshape(DB, 128, NS).transpose(1, 0, 2).reshape(128, DB * NS)),
        "gw2": np.ascontiguousarray(gate_w.T.reshape(DB, 128, E).transpose(1, 0, 2).reshape(128, DB * E)),
        "eb128": np.ascontiguousarray(np.broadcast_to(ebias[None, :], (128, E))),
        "ones1": np.ones((1, 128), np.float32),
        "ident": np.eye(128, dtype=np.float32),
        "c128": np.ascontiguousarray(np.broadcast_to(
            np.array([EPS, -0.5, 1.0, 0.0], np.float32)[None, :], (128, 4))),
        "sel16": np.ascontiguousarray(np.kron(np.eye(NS, dtype=np.float32), np.ones((1, 128), np.float32))).astype(ml_dtypes.bfloat16),
        "wgT": np.ascontiguousarray(w_gate.transpose(0, 2, 1)),
        "wuT": np.ascontiguousarray(w_up.transpose(0, 2, 1)),
        "wdT": np.ascontiguousarray(w_down.transpose(0, 2, 1)),
        "swgT": np.ascontiguousarray(sw_gate.T),
        "swuT": np.ascontiguousarray(sw_up.T),
        "swdT": np.ascontiguousarray(sw_down.T),
    }
    in_maps = []
    for c in range(8):
        b, half = c // 2, c % 2
        m = dict(com)
        m["x0"] = np.ascontiguousarray(x[b])
        m["toff"] = np.array([[half * TOK]], np.int32)
        in_maps.append(m)
    return in_maps


def kernel(**inputs):
    in_maps = _host_layouts(inputs)
    nc = build_program(dbg=False)
    res = run_bass_kernel_spmd(nc, in_maps, core_ids=list(range(8)))
    out = np.empty((B, S, D), np.float32)
    for c in range(8):
        b, half = c // 2, c % 2
        out[b, half * TOK:(half + 1) * TOK, :] = res.results[c]["out"]
    return out


# version 06
# speedup vs baseline: 1.5468x; 1.0944x over previous
import numpy as np
import ml_dtypes
from contextlib import ExitStack

import concourse.bass as bass
import concourse.mybir as mybir
from concourse.bass_utils import run_bass_kernel_spmd

F32 = mybir.dt.float32
BF16 = mybir.dt.bfloat16
I32 = mybir.dt.int32
AF = mybir.ActivationFunctionType
ALU = mybir.AluOpType

B, S, D = 4, 4096, 2048
NS = 16
R = 64
E = 16
I = 1024
CH = 512
NCH = S // CH
DB = D // 128
TT = CH // 128
TOK = S // 2
HPT = 1024
EPS = 1e-5

ENGS = ("sync", "gpsimd", "vector", "scalar", "tensor")
XLN = False
XEV = "scalar"
XBAR = True
XBENGS = ENGS
XDUM = True


class _SkipPhaseB(Exception):
    pass


class Prog:

    def __init__(self):
        self.q = {e: [] for e in ENGS}
        self.cnt = {}

    def sem(self, name):
        self.cnt.setdefault(name, 0)
        return name

    def op(self, eng, fn, inc=None, k=1):
        if inc is not None:
            self.cnt[inc] += k
        self.q[eng].append(["op", fn, inc, k, self.cnt[inc] if inc else 0])

    def wait(self, eng, sem, val=None):
        if val is None:
            val = self.cnt[sem]
        if val > 0:
            self.q[eng].append(("wait", sem, val))

    def val(self, sem):
        return self.cnt[sem]

    def replay(self, eng_name, handle, sems, start, end):
        for item in self.q[eng_name][start:end]:
            if item[0] == "wait":
                handle.wait_ge(sems[item[1]], item[2])
            else:
                _, fn, inc, k, _pv = item
                instr = fn(handle)
                if inc is not None:
                    instr.then_inc(sems[inc], k)

    def marks(self):
        return {e: len(self.q[e]) for e in ENGS}

    def mark_last(self, eng, own_sem):
        for i in range(len(self.q[eng]) - 1, -1, -1):
            it = self.q[eng][i]
            if it[0] == "op":
                if it[2] is None:
                    self.cnt[own_sem] += 1
                    it[2] = own_sem
                    it[3] = 1
                    it[4] = self.cnt[own_sem]
                return (it[2], it[4])
        return None

    def truncate(self, n_ops):
        cnt = {s: 0 for s in self.cnt}
        ptr = {e: 0 for e in ENGS}
        done = 0
        while done < n_ops:
            prog = False
            for e in ENGS:
                while ptr[e] < len(self.q[e]) and done < n_ops:
                    it = self.q[e][ptr[e]]
                    if it[0] == "wait":
                        if cnt[it[1]] < it[2]:
                            break
                    else:
                        if it[2] is not None:
                            cnt[it[2]] += it[3]
                        done += 1
                    ptr[e] += 1
                    prog = True
            if not prog:
                break
        for e in ENGS:
            self.q[e] = self.q[e][:ptr[e]]
        return done


def build_program(dbg=False, n_chunks=NCH, phase_b=True, stage=99, trunc=None):
    nc = bass.Bass("TRN2", target_bir_lowering=False)

    x0 = nc.dram_tensor("x0", [S, D], F32, kind="ExternalInput")
    wx2 = nc.dram_tensor("wx2", [128, DB * 96], F32, kind="ExternalInput")
    wdt = nc.dram_tensor("wdt", [R, D], F32, kind="ExternalInput")
    bdt2 = nc.dram_tensor("bdt2", [128, DB], F32, kind="ExternalInput")
    dp2 = nc.dram_tensor("dp2", [128, DB], F32, kind="ExternalInput")
    alog2 = nc.dram_tensor("alog2", [128, DB * NS], F32, kind="ExternalInput")
    gw2 = nc.dram_tensor("gw2", [128, DB * E], F32, kind="ExternalInput")
    eb128 = nc.dram_tensor("eb128", [128, E], F32, kind="ExternalInput")
    ones1 = nc.dram_tensor("ones1", [1, 128], F32, kind="ExternalInput")
    ident = nc.dram_tensor("ident", [128, 128], F32, kind="ExternalInput")
    sel16 = nc.dram_tensor("sel16", [NS, NS * 128], BF16, kind="ExternalInput")
    toff = nc.dram_tensor("toff", [1, 1], I32, kind="ExternalInput")
    c128 = nc.dram_tensor("c128", [128, 4], F32, kind="ExternalInput")
    wgT = nc.dram_tensor("wgT", [E, D, I], F32, kind="ExternalInput")
    wuT = nc.dram_tensor("wuT", [E, D, I], F32, kind="ExternalInput")
    wdT = nc.dram_tensor("wdT", [E, I, D], F32, kind="ExternalInput")
    swgT = nc.dram_tensor("swgT", [D, I], F32, kind="ExternalInput")
    swuT = nc.dram_tensor("swuT", [D, I], F32, kind="ExternalInput")
    swdT = nc.dram_tensor("swdT", [I, D], F32, kind="ExternalInput")

    out = nc.dram_tensor("out", [TOK, D], F32, kind="ExternalOutput")
    scr_kind = "ExternalOutput" if dbg else "Internal"
    x1_scr = nc.dram_tensor("x1_scr", [S, D], F32, kind=scr_kind)
    cw_scr = nc.dram_tensor("cw_scr", [S, E], F32, kind=scr_kind)
    v_scr = nc.dram_tensor("v_scr", [DB, 128, S], BF16, kind=scr_kind)
    if dbg:
        y_scr = nc.dram_tensor("y_scr", [DB, 128, S], F32, kind="ExternalOutput")
        u_scr = nc.dram_tensor("u_scr", [DB, 128, S], F32, kind="ExternalOutput")
        dt_scr = nc.dram_tensor("dt_scr", [DB, 128, S], F32, kind="ExternalOutput")
        del_scr = nc.dram_tensor("del_scr", [R, S], F32, kind="ExternalOutput")
        stat_scr = nc.dram_tensor("stat_scr", [128, 32], F32, kind="ExternalOutput")
        rout_scr = nc.dram_tensor("rout_scr", [TOK, D], F32, kind="ExternalOutput")
        dump_scr = nc.dram_tensor("dump_scr", [128, 2048], F32, kind="ExternalOutput")
        bb_scr = nc.dram_tensor("bb_scr", [NS, S], BF16, kind="ExternalOutput")

    p = Prog()
    q_s = p.sem("q_s")
    q_g = p.sem("q_g")
    q_o = p.sem("q_o")
    sv = p.sem("sv")
    sa = p.sem("sa")
    st = p.sem("st")
    sg = p.sem("sg")
    ss = p.sem("ss")
    allsems = [q_s, q_g, q_o, sv, sa, st, sg, ss]

    ESEM = {"sync": ss, "gpsimd": sg, "vector": sv, "scalar": sa, "tensor": st}

    def vws(fn):
        p.op("vector", fn, inc=sv)
        p.wait("vector", sv)

    def aws_(fn):
        p.op("scalar", fn, inc=sa)
        p.wait("scalar", sa)

    def barrier():
        if not XBAR:
            return
        p.wait("sync", q_s)
        p.wait("sync", q_o)
        p.wait("gpsimd", q_g)
        snap = {e: p.mark_last(e, ESEM[e]) for e in ENGS}
        for e in ENGS:
            for o in ENGS:
                if o != e and snap[o] is not None:
                    p.wait(e, ESEM[o], snap[o][1] if snap[o][0] == ESEM[o] else None)                         if False else None
        for e in ENGS:
            for o in ENGS:
                if o != e and snap[o] is not None:
                    p.wait(e, snap[o][0], snap[o][1])

    with ExitStack() as ctx:
        en = ctx.enter_context
        semh = {s: en(nc.semaphore(s)) for s in allsems}
        r_t = en(nc.sync.register("r_t"))
        r_a = en(nc.sync.register("r_a"))
        r_c = en(nc.sync.register("r_c"))

        s_wx = en(nc.sbuf_tensor("s_wx", [128, DB * 96], F32))
        s_wdt = en(nc.sbuf_tensor("s_wdt", [R, D], F32))
        s_bdt = en(nc.sbuf_tensor("s_bdt", [128, DB], F32))
        s_dp = en(nc.sbuf_tensor("s_dp", [128, DB], F32))
        s_nA = en(nc.sbuf_tensor("s_nA", [128, DB * NS], F32))
        s_gw = en(nc.sbuf_tensor("s_gw", [128, DB * E], F32))
        s_eb = en(nc.sbuf_tensor("s_eb", [128, E], F32))
        s_ones = en(nc.sbuf_tensor("s_ones", [1, 128], F32))
        s_id = en(nc.sbuf_tensor("s_id", [128, 128], F32))
        s_sel = en(nc.sbuf_tensor("s_sel", [NS, NS * 128], BF16))
        s_toff = en(nc.sbuf_tensor("s_toff", [1, 1], I32))
        s_c = en(nc.sbuf_tensor("s_c", [128, 4], F32))
        s_hcar = en(nc.sbuf_tensor("s_hcar", [128, DB * NS], F32))

        P = [en(nc.psum_tensor(f"P{i}", [128, 512], F32)) for i in range(8)]

        for name, dst, src in (
            ("wx", s_wx, wx2), ("wdt", s_wdt, wdt), ("bdt", s_bdt, bdt2),
            ("dp", s_dp, dp2), ("nA", s_nA, alog2), ("gw", s_gw, gw2),
            ("eb", s_eb, eb128), ("on", s_ones, ones1), ("id", s_id, ident),
            ("sel", s_sel, sel16), ("tof", s_toff, toff), ("c", s_c, c128),
        ):
            p.op("sync", lambda h, d=dst, s=src: h.dma_start(out=d[:], in_=s.ap()), inc=q_s, k=16)
        p.wait("scalar", q_s)
        p.op("scalar", lambda h: h.activation(s_nA[:], s_nA[:], AF.Exp), inc=sa)
        p.wait("vector", sa)
        p.op("vector", lambda h: h.tensor_scalar_mul(s_nA[:], s_nA[:], -1.0), inc=sv)
        p.op("vector", lambda h: h.memset(s_hcar[:], 0.0), inc=sv)
        p.wait("scalar", sv)

        with ExitStack() as actx:
            an = actx.enter_context
            s_tm = an(nc.sbuf_tensor("s_tm", [128, TT * D], F32))
            s_fa = an(nc.sbuf_tensor("s_fa", [128, DB * CH], F32))
            s_fb = an(nc.sbuf_tensor("s_fb", [128, DB * CH], F32))
            s_fc = an(nc.sbuf_tensor("s_fc", [128, DB * CH], F32))
            s_fd = an(nc.sbuf_tensor("s_fd", [128, DB * CH], F32))
            s_a2 = [an(nc.sbuf_tensor(f"s_a{i}", [128, CH], F32)) for i in range(2)]
            s_b2 = [an(nc.sbuf_tensor(f"s_b{i}", [128, CH], F32)) for i in range(2)]
            s_h2 = [an(nc.sbuf_tensor(f"s_hh{i}", [128, CH], F32)) for i in range(2)]
            s_hc = an(nc.sbuf_tensor("s_hc", [128, CH], F32))
            s_del = an(nc.sbuf_tensor("s_del", [R, CH], F32))
            s_bB = an(nc.sbuf_tensor("s_bB", [NS, CH], BF16))
            s_bC = an(nc.sbuf_tensor("s_bC", [NS, CH], BF16))
            s_vbf = an(nc.sbuf_tensor("s_vbf", [128, 2 * CH], BF16))
            s_stat = an(nc.sbuf_tensor("s_stat", [128, 32], F32))
            s_sg4 = an(nc.sbuf_tensor("s_sg4", [128, TT * E], F32))
            s_rt = an(nc.sbuf_tensor("s_rt", [128, E], F32))
            s_m8 = an(nc.sbuf_tensor("s_m8", [128, 8], F32))
            s_cw = an(nc.sbuf_tensor("s_cw", [128, TT * E], F32))

            def ln_stats(src_buf, src_off, dump_buf, dump_off, ncols):
                for a in range(TT):
                    p.op("scalar", lambda h, a=a: h.activation(
                        dump_buf[:, dump_off + a * ncols: dump_off + (a + 1) * ncols],
                        src_buf[:, src_off + a * ncols: src_off + (a + 1) * ncols],
                        AF.Square), inc=sa)
                    p.op("vector", lambda h, a=a: h.reduce_sum(
                        s_stat[:, a:a + 1],
                        src_buf[:, src_off + a * ncols: src_off + (a + 1) * ncols],
                        mybir.AxisListType.X), inc=sv)
                p.wait("vector", sa)
                for a in range(TT):
                    p.op("vector", lambda h, a=a: h.reduce_sum(
                        s_stat[:, 4 + a:5 + a],
                        dump_buf[:, dump_off + a * ncols: dump_off + (a + 1) * ncols],
                        mybir.AxisListType.X), inc=(sv if a == TT - 1 else None))
                p.wait("vector", sv)
                vws(lambda h: h.tensor_scalar_mul(
                    s_stat[:, 8:12], s_stat[:, 0:4], 1.0 / ncols))
                vws(lambda h: h.tensor_mul(
                    s_stat[:, 12:16], s_stat[:, 8:12], s_stat[:, 8:12]))
                vws(lambda h: h.tensor_scalar_mul(
                    s_stat[:, 16:20], s_stat[:, 4:8], 1.0 / ncols))
                vws(lambda h: h.tensor_sub(
                    s_stat[:, 16:20], s_stat[:, 16:20], s_stat[:, 12:16]))
                p.wait("scalar", sv)
                aws_(lambda h: h.activation(
                    s_stat[:, 24:28], s_stat[:, 16:20], AF.Ln, bias=s_c[:, 0:1]))
                aws_(lambda h: h.activation(
                    s_stat[:, 20:24], s_stat[:, 24:28], AF.Exp, scale=s_c[:, 1:2]))
                p.wait("vector", sa)

            scan_hist = []
            for c in range(n_chunks):
                r = c * CH
                p.op("sync", lambda h, r=r: h.dma_start(
                    out=s_tm[:].rearrange("p (a d) -> p a d", a=TT),
                    in_=x0.ap()[r:r + CH, :].rearrange("(a p) d -> p a d", p=128)),
                    inc=q_s, k=16)
                p.wait("scalar", q_s)
                p.wait("vector", q_s)
                if not XLN:
                    ln_stats(s_tm, 0, s_fd, 0, D)
                    for a in range(TT):
                        p.op("vector", lambda h, a=a: h.tensor_scalar(
                            s_tm[:, a * D:(a + 1) * D], s_tm[:, a * D:(a + 1) * D],
                            s_stat[:, 8 + a:9 + a], s_stat[:, 20 + a:21 + a],
                            ALU.subtract, ALU.mult), inc=sv)
                if stage <= 1:
                    barrier()
                    continue
                if dbg and c == 0:
                    p.wait("sync", sv)
                    p.op("sync", lambda h: h.dma_start(out=stat_scr.ap(), in_=s_stat[:]), inc=q_s, k=16)
                    p.op("sync", lambda h: h.dma_start(out=dump_scr.ap(), in_=s_fd[:, 0:D]), inc=q_s, k=16)
                p.wait("tensor", sv)
                for db in range(DB):
                    if c > 0 or db > 1:
                        p.wait("tensor", sa)
                    for a in range(TT):
                        p.op("tensor", lambda h, db=db, a=a: h.transpose(
                            P[db % 2][:, a * 128:(a + 1) * 128],
                            s_tm[:, a * D + db * 128: a * D + (db + 1) * 128],
                            s_id[:]), inc=(st if a == TT - 1 else None))
                    p.wait(XEV, st)
                    if XEV == "scalar":
                        p.op(XEV, lambda h, db=db: h.copy(
                            s_fa[:, db * CH:(db + 1) * CH], P[db % 2][:]), inc=sa)
                    else:
                        p.op(XEV, lambda h, db=db: h.tensor_copy(
                            s_fa[:, db * CH:(db + 1) * CH], P[db % 2][:]), inc=sa)
                if stage <= 2:
                    barrier()
                    continue
                p.wait("tensor", sa)
                for db in range(DB):
                    p.op("tensor", lambda h, db=db: h.matmul(
                        P[2][0:R, :], s_wx[:, db * 96: db * 96 + R],
                        s_fa[:, db * CH:(db + 1) * CH],
                        start=(db == 0), stop=(db == DB - 1)),
                        inc=(st if db == DB - 1 else None))
                p.wait("scalar", st)
                p.op("scalar", lambda h: h.copy(s_del[:], P[2][0:R, :]), inc=sa)
                for db in range(DB):
                    p.op("tensor", lambda h, db=db: h.matmul(
                        P[3][0:NS, :], s_wx[:, db * 96 + R: db * 96 + R + NS],
                        s_fa[:, db * CH:(db + 1) * CH],
                        start=(db == 0), stop=(db == DB - 1)),
                        inc=(st if db == DB - 1 else None))
                p.wait("scalar", st)
                p.op("scalar", lambda h: h.copy(s_bB[:], P[3][0:NS, :]), inc=sa)
                p.wait("tensor", sa)
                for db in range(DB):
                    p.op("tensor", lambda h, db=db: h.matmul(
                        P[2][0:NS, :], s_wx[:, db * 96 + R + NS: db * 96 + R + 2 * NS],
                        s_fa[:, db * CH:(db + 1) * CH],
                        start=(db == 0), stop=(db == DB - 1)),
                        inc=(st if db == DB - 1 else None))
                p.wait("scalar", st)
                p.op("scalar", lambda h: h.copy(s_bC[:], P[2][0:NS, :]), inc=sa)
                if stage <= 3:
                    barrier()
                    continue
                p.wait("tensor", sa)
                for db in range(DB):
                    p.op("tensor", lambda h, db=db: h.matmul(
                        P[2 + db % 2][:], s_wdt[:, db * 128:(db + 1) * 128],
                        s_del[:], start=True, stop=True), inc=st)
                    p.wait("scalar", st)
                    p.op("scalar", lambda h, db=db: h.activation(
                        s_fb[:, db * CH:(db + 1) * CH], P[2 + db % 2][:],
                        AF.Exp, bias=s_bdt[:, db:db + 1]))
                    p.op("scalar", lambda h, db=db: h.activation(
                        s_fb[:, db * CH:(db + 1) * CH],
                        s_fb[:, db * CH:(db + 1) * CH],
                        AF.Ln, bias=s_c[:, 2:3]), inc=sa)
                    if db < DB - 1:
                        p.wait("tensor", sa)
                if stage <= 4:
                    barrier()
                    continue
                if dbg:
                    p.wait("sync", sa)
                    p.op("sync", lambda h, r=r: h.dma_start(
                        out=u_scr.ap().rearrange("j p s -> p j s")[:, :, r:r + CH],
                        in_=s_fa[:].rearrange("p (j t) -> p j t", j=DB)), inc=q_s, k=16)
                    p.op("sync", lambda h, r=r: h.dma_start(
                        out=dt_scr.ap().rearrange("j p s -> p j s")[:, :, r:r + CH],
                        in_=s_fb[:].rearrange("p (j t) -> p j t", j=DB)), inc=q_s, k=16)
                    p.op("sync", lambda h, r=r: h.dma_start(
                        out=del_scr.ap()[:, r:r + CH], in_=s_del[:]), inc=q_s, k=16)
                    p.op("sync", lambda h, r=r: h.dma_start(
                        out=bb_scr.ap()[:, r:r + CH], in_=s_bB[:]), inc=q_s, k=16)
                p.wait("vector", sa)
                for db in range(DB):
                    p.op("vector", lambda h, db=db: h.tensor_mul(
                        s_fc[:, db * CH:(db + 1) * CH],
                        s_fb[:, db * CH:(db + 1) * CH],
                        s_fa[:, db * CH:(db + 1) * CH]), inc=sv)
                for db in range(DB):
                    for n in range(NS):
                        idx = db * NS + n
                        pb, pc = P[4 + idx % 2], P[6 + idx % 2]
                        sA, sB = s_a2[idx % 2], s_b2[idx % 2]
                        sH = s_h2[idx % 2]
                        gidx = len(scan_hist)
                        if gidx >= 2:
                            p.wait("scalar", sv, scan_hist[gidx - 2])
                            if idx >= 2:
                                p.op("scalar", lambda h, idx=idx: h.copy(
                                    s_hcar[:, idx - 2:idx - 1],
                                    s_h2[idx % 2][:, CH - 1:CH]))
                            p.wait("tensor", sv, scan_hist[gidx - 2])
                        p.op("scalar", lambda h, db=db, idx=idx, sA=sA: h.activation(
                            sA[:], s_fb[:, db * CH:(db + 1) * CH],
                            AF.Exp, scale=s_nA[:, idx:idx + 1]), inc=sa)
                        p.op("tensor", lambda h, n=n, pb=pb: h.matmul(
                            pb[:], s_sel[:, n * 128:(n + 1) * 128], s_bB[:],
                            start=True, stop=True))
                        p.op("tensor", lambda h, n=n, pc=pc: h.matmul(
                            pc[:], s_sel[:, n * 128:(n + 1) * 128], s_bC[:],
                            start=True, stop=True), inc=st)
                        p.wait("vector", sa)
                        p.wait("vector", st)
                        p.op("vector", lambda h, db=db, sB=sB, pb=pb: h.tensor_mul(
                            sB[:], s_fc[:, db * CH:(db + 1) * CH], pb[:]))
                        p.op("vector", lambda h, sA=sA, sB=sB, sH=sH, idx=idx: h.tensor_tensor_scan(
                            sH[:], sA[:], sB[:], s_hcar[:, idx:idx + 1],
                            ALU.mult, ALU.add))
                        if n == 0:
                            p.op("vector", lambda h, db=db, pc=pc, sH=sH: h.tensor_mul(
                                s_fd[:, db * CH:(db + 1) * CH], sH[:], pc[:]), inc=sv)
                        else:
                            p.op("vector", lambda h, pc=pc, sH=sH: h.tensor_mul(
                                s_hc[:], sH[:], pc[:]))
                            p.op("vector", lambda h, db=db: h.tensor_add(
                                s_fd[:, db * CH:(db + 1) * CH],
                                s_fd[:, db * CH:(db + 1) * CH], s_hc[:]), inc=sv)
                        scan_hist.append(p.val(sv))
                    p.op("vector", lambda h, db=db: h.scalar_tensor_tensor(
                        s_fd[:, db * CH:(db + 1) * CH],
                        s_fa[:, db * CH:(db + 1) * CH], s_dp[:, db:db + 1],
                        s_fd[:, db * CH:(db + 1) * CH],
                        ALU.mult, ALU.add), inc=sv)
                for back in (2, 1):
                    p.wait("scalar", sv, scan_hist[len(scan_hist) - back])
                    p.op("scalar", lambda h, back=back: h.copy(
                        s_hcar[:, DB * NS - back: DB * NS - back + 1],
                        s_h2[back % 2][:, CH - 1:CH]), inc=sa)
                v_y_done = p.val(sv)
                if dbg:
                    p.wait("sync", sv, v_y_done)
                    p.op("sync", lambda h, r=r: h.dma_start(
                        out=y_scr.ap().rearrange("j p s -> p j s")[:, :, r:r + CH],
                        in_=s_fd[:].rearrange("p (j t) -> p j t", j=DB)), inc=q_s, k=16)
                if stage <= 5:
                    barrier()
                    continue
                p.wait("tensor", sv, v_y_done)
                for a in range(TT):
                    for dq in range(4):
                        p.wait("tensor", sa)
                        for j in range(4):
                            db = dq * 4 + j
                            p.op("tensor", lambda h, a=a, dq=dq, j=j, db=db: h.transpose(
                                P[(a * 4 + dq) % 2][:, j * 128:(j + 1) * 128],
                                s_fd[:, db * CH + a * 128: db * CH + (a + 1) * 128],
                                s_id[:]), inc=(st if j == 3 else None))
                        p.wait("scalar", st)
                        p.op("scalar", lambda h, a=a, dq=dq: h.copy(
                            s_tm[:, a * D + dq * 512: a * D + (dq + 1) * 512],
                            P[(a * 4 + dq) % 2][:]), inc=sa)
                p.wait("sync", sa)
                p.op("sync", lambda h, r=r: h.dma_start(
                    out=s_fb[:].rearrange("p (a d) -> p a d", a=TT),
                    in_=x0.ap()[r:r + CH, :].rearrange("(a p) d -> p a d", p=128)),
                    inc=q_s, k=16)
                p.wait("scalar", sa)
                p.wait("vector", sa)
                ln_stats(s_tm, 0, s_fa, 0, D)
                p.wait("vector", q_s)
                for a in range(TT):
                    p.op("vector", lambda h, a=a: h.tensor_scalar(
                        s_fc[:, a * D:(a + 1) * D], s_tm[:, a * D:(a + 1) * D],
                        s_stat[:, 8 + a:9 + a], s_stat[:, 20 + a:21 + a],
                        ALU.subtract, ALU.mult))
                    p.op("vector", lambda h, a=a: h.tensor_add(
                        s_fc[:, a * D:(a + 1) * D], s_fc[:, a * D:(a + 1) * D],
                        s_fb[:, a * D:(a + 1) * D]), inc=sv)
                v_x1 = p.val(sv)
                p.wait("sync", sv, v_x1)
                p.op("sync", lambda h, r=r: h.dma_start(
                    out=x1_scr.ap()[r:r + CH, :].rearrange("(a p) d -> p a d", p=128),
                    in_=s_fc[:, 0:TT * D].rearrange("p (a d) -> p a d", a=TT)), inc=q_s, k=16)
                if stage <= 6:
                    barrier()
                    continue
                p.wait("scalar", sv)
                ln_stats(s_fc, 0, s_fa, 0, D)
                for a in range(TT):
                    p.op("vector", lambda h, a=a: h.tensor_scalar(
                        s_fd[:, a * D:(a + 1) * D], s_fc[:, a * D:(a + 1) * D],
                        s_stat[:, 8 + a:9 + a], s_stat[:, 20 + a:21 + a],
                        ALU.subtract, ALU.mult), inc=sv)
                p.wait("tensor", sv)
                for db in range(DB):
                    p.wait("tensor", sa)
                    p.wait("tensor", sv)
                    for a in range(TT):
                        p.op("tensor", lambda h, db=db, a=a: h.transpose(
                            P[db % 2][:, a * 128:(a + 1) * 128],
                            s_fd[:, a * D + db * 128: a * D + (db + 1) * 128],
                            s_id[:]), inc=(st if a == TT - 1 else None))
                    p.wait("scalar", st)
                    p.op("scalar", lambda h, db=db: h.copy(
                        s_fa[:, db * CH:(db + 1) * CH], P[db % 2][:]), inc=sa)
                    p.wait("vector", sa)
                    if db >= 2:
                        p.wait("vector", q_s)
                    p.op("vector", lambda h, db=db: h.tensor_copy(
                        s_vbf[:, (db % 2) * CH:(db % 2 + 1) * CH],
                        s_fa[:, db * CH:(db + 1) * CH]), inc=sv)
                    if db % 2 == 1:
                        p.wait("sync", sv)
                        p.op("sync", lambda h, db=db, r=r: h.dma_start(
                            out=v_scr.ap().rearrange("j p s -> p j s")[:, db - 1:db + 1, r:r + CH],
                            in_=s_vbf[:].rearrange("p (j t) -> p j t", j=2)),
                            inc=q_s, k=16)
                if stage <= 7:
                    barrier()
                    continue
                p.wait("tensor", sa)
                for a in range(TT):
                    pl = P[2 + a % 2]
                    if a > 1:
                        p.wait("tensor", sa)
                    for db in range(DB):
                        p.op("tensor", lambda h, a=a, db=db, pl=pl: h.matmul(
                            pl[:, 0:E], s_fa[:, db * CH + a * 128: db * CH + (a + 1) * 128],
                            s_gw[:, db * E:(db + 1) * E],
                            start=(db == 0), stop=(db == DB - 1)),
                            inc=(st if db == DB - 1 else None))
                    p.wait("scalar", st)
                    p.op("scalar", lambda h, a=a, pl=pl: h.activation(
                        s_sg4[:, a * E:(a + 1) * E], pl[:, 0:E], AF.Sigmoid), inc=sa)
                p.wait("vector", sa)
                for a in range(TT):
                    vws(lambda h, a=a: h.tensor_add(
                        s_rt[:], s_sg4[:, a * E:(a + 1) * E], s_eb[:]))
                    vws(lambda h: h.max(s_m8[:], s_rt[:]))
                    vws(lambda h: h.tensor_scalar(
                        s_rt[:], s_rt[:], s_m8[:, 3:4], None, ALU.is_ge))
                    vws(lambda h, a=a: h.tensor_mul(
                        s_rt[:], s_rt[:], s_sg4[:, a * E:(a + 1) * E]))
                    vws(lambda h: h.reduce_sum(
                        s_m8[:, 4:5], s_rt[:], mybir.AxisListType.X))
                    vws(lambda h: h.reciprocal(s_m8[:, 5:6], s_m8[:, 4:5]))
                    vws(lambda h, a=a: h.tensor_scalar_mul(
                        s_cw[:, a * E:(a + 1) * E], s_rt[:], s_m8[:, 5:6]))
                p.wait("sync", sv)
                p.op("sync", lambda h, r=r: h.dma_start(
                    out=cw_scr.ap()[r:r + CH, :].rearrange("(a p) e -> p a e", p=128),
                    in_=s_cw[:].rearrange("p (a e) -> p a e", a=TT)), inc=q_s, k=16)
                barrier()

        if not phase_b:
            if XDUM:
                p.op("sync", lambda h: h.dma_start(out=out.ap()[0:1, 0:DB * E], in_=s_gw[0:1, :]), inc=q_o, k=16)
                p.wait("sync", q_o)
            if trunc is not None:
                print("truncated to", p.truncate(trunc), "ops")
            ends = p.marks()
            with nc.Block("main") as blk:
                for eng in ENGS:
                    def body(h, eng=eng):
                        p.replay(eng, h, semh, 0, ends[eng])
                    getattr(blk, eng)(body)
            nc._prog = p
            return nc
        with ExitStack() as bctx:
            bn = bctx.enter_context
            s_v = bn(nc.sbuf_tensor("s_v", [128, DB * HPT], BF16))
            s_rout = bn(nc.sbuf_tensor("s_rout", [128, 8 * D], F32))
            s_act = bn(nc.sbuf_tensor("s_act", [128, 8 * HPT], BF16))
            s_wg2 = [bn(nc.sbuf_tensor(f"s_wg{i}", [128, DB * 128], BF16)) for i in range(2)]
            s_wu2 = [bn(nc.sbuf_tensor(f"s_wu{i}", [128, DB * 128], BF16)) for i in range(2)]
            s_wd = bn(nc.sbuf_tensor("s_wd", [128, 8 * D], BF16))
            s_sgl = [bn(nc.sbuf_tensor(f"s_sgl{i}", [128, 512], F32)) for i in range(2)]
            s_cwb = bn(nc.sbuf_tensor("s_cwb", [128, 8 * E], F32))
            s_x1t = bn(nc.sbuf_tensor("s_x1t", [128, D], F32))
            s_outt = bn(nc.sbuf_tensor("s_outt", [128, D], F32))
            s_stB = bn(nc.sbuf_tensor("s_stB", [128, 8], F32))

            for eng in ENGS:
                p.wait(eng, q_s)
            p.op("sync", lambda h: h.reg_load(r_t, s_toff[0:1, 0:1]))

            gctr = 0
            qg_hist = {}
            mm1_hist = {}
            stt_hist = []
            for hp in range(2):
                p.op("sync", lambda h, hp=hp: h.reg_mov(r_a, hp * HPT))
                p.op("sync", lambda h: h.reg_add(r_a, r_a, r_t))
                p.op("sync", lambda h: h.dma_start(
                    out=s_v[:],
                    in_=bass.AP(v_scr, r_a, [[S, 128], [128 * S, DB], [1, HPT]])),
                    inc=q_s, k=16)
                p.op("sync", lambda h: h.reg_mul(r_c, r_a, E))
                p.op("sync", lambda h: h.dma_start(
                    out=s_cwb[:],
                    in_=bass.AP(cw_scr, r_c, [[E, 128], [128 * E, 8], [1, E]])),
                    inc=q_s, k=16)
                if hp == 1:
                    p.wait("vector", q_o)
                    p.wait("scalar", q_o)
                p.op("vector", lambda h: h.memset(s_rout[:], 0.0), inc=sv)
                p.wait("tensor", q_g)
                p.wait("tensor", q_s)
                p.wait("tensor", sv)
                p.wait("scalar", q_g)
                p.wait("scalar", q_s)

                def emit_load(g):
                    e2, ib2 = (g % ((E + 1) * 8)) // 8, g % 8
                    sh2 = (e2 == E)
                    wg_s = swgT if sh2 else wgT
                    wu_s = swuT if sh2 else wuT
                    eb2 = 0 if sh2 else e2 * D * I
                    p.op("gpsimd", lambda h, ib2=ib2, wg_s=wg_s, eb2=eb2: h.dma_start(
                        out=s_wg2[ib2 % 2][:].rearrange("p (j i) -> p j i", j=DB),
                        in_=bass.AP(wg_s, eb2 + ib2 * 128,
                                    [[I, 128], [128 * I, DB], [1, 128]])),
                        inc=q_g, k=16)
                    p.op("gpsimd", lambda h, ib2=ib2, wu_s=wu_s, eb2=eb2: h.dma_start(
                        out=s_wu2[ib2 % 2][:].rearrange("p (j i) -> p j i", j=DB),
                        in_=bass.AP(wu_s, eb2 + ib2 * 128,
                                    [[I, 128], [128 * I, DB], [1, 128]])),
                        inc=q_g, k=16)
                    qg_hist[g] = p.val(q_g)

                GMAX = 2 * (E + 1) * 8 - 1
                for e in range(E + 1):
                    shared = (e == E)
                    wd_src = swdT if shared else wdT
                    dbase = 0 if shared else e * I * D

                    if gctr == 0:
                        emit_load(0)
                        emit_load(1)
                    p.wait("gpsimd", st)
                    for half in range(2):
                        p.op("gpsimd", lambda h, half=half, wd_src=wd_src, dbase=dbase: h.dma_start(
                            out=s_wd[:, half * 4 * D:(half + 1) * 4 * D].rearrange(
                                "p (j d) -> p j d", j=4),
                            in_=bass.AP(wd_src, dbase + half * 4 * 128 * D,
                                        [[D, 128], [128 * D, 4], [1, D]])),
                            inc=q_g, k=16)
                    for ib in range(8):
                        g = gctr + ib
                        p.wait("tensor", q_g, qg_hist[g])
                        if g > 0:
                            p.wait("tensor", sv)
                        for tch in range(2):
                            for db in range(DB):
                                p.op("tensor", lambda h, ib=ib, tch=tch, db=db: h.matmul(
                                    P[4 + tch][:], s_wg2[ib % 2][:, db * 128:(db + 1) * 128],
                                    s_v[:, db * HPT + tch * 512: db * HPT + (tch + 1) * 512],
                                    start=(db == 0), stop=(db == DB - 1)))
                            for db in range(DB):
                                p.op("tensor", lambda h, ib=ib, tch=tch, db=db: h.matmul(
                                    P[6 + tch][:], s_wu2[ib % 2][:, db * 128:(db + 1) * 128],
                                    s_v[:, db * HPT + tch * 512: db * HPT + (tch + 1) * 512],
                                    start=(db == 0), stop=(db == DB - 1)),
                                    inc=(st if db == DB - 1 else None))
                            p.wait("scalar", st)
                            p.op("scalar", lambda h, tch=tch: h.activation(
                                s_sgl[tch][:], P[4 + tch][:], AF.Silu), inc=sa)
                            p.wait("vector", sa)
                            p.op("vector", lambda h, ib=ib, tch=tch: h.tensor_mul(
                                s_act[:, ib * HPT + tch * 512: ib * HPT + (tch + 1) * 512],
                                P[6 + tch][:], s_sgl[tch][:]), inc=sv)
                        mm1_hist[g] = p.val(st)
                        if g + 2 <= GMAX:
                            p.wait("gpsimd", st, mm1_hist[g])
                            emit_load(g + 2)
                    gctr += 8
                    v_act_done = p.val(sv)
                    p.wait("tensor", sv, v_act_done)
                    p.wait("tensor", q_g)
                    for tsub in range(8):
                        for dch in range(4):
                            k = (tsub * 4 + dch) % 2
                            m2 = len(stt_hist)
                            if m2 >= 2:
                                p.wait("tensor", sv, stt_hist[m2 - 2])
                            for ib in range(8):
                                p.op("tensor", lambda h, tsub=tsub, dch=dch, ib=ib, k=k: h.matmul(
                                    P[k][:], s_act[:, ib * HPT + tsub * 128:
                                                   ib * HPT + (tsub + 1) * 128],
                                    s_wd[:, ib * D + dch * 512: ib * D + (dch + 1) * 512],
                                    start=(ib == 0), stop=(ib == 7)),
                                    inc=(st if ib == 7 else None))
                            p.wait("vector", st)
                            if shared:
                                p.op("vector", lambda h, tsub=tsub, dch=dch, k=k: h.scalar_tensor_tensor(
                                    s_rout[:, tsub * D + dch * 512: tsub * D + (dch + 1) * 512],
                                    P[k][:], s_c[:, 2:3],
                                    s_rout[:, tsub * D + dch * 512: tsub * D + (dch + 1) * 512],
                                    ALU.mult, ALU.add), inc=sv)
                            else:
                                p.op("vector", lambda h, tsub=tsub, dch=dch, k=k, e=e: h.scalar_tensor_tensor(
                                    s_rout[:, tsub * D + dch * 512: tsub * D + (dch + 1) * 512],
                                    P[k][:], s_cwb[:, tsub * E + e: tsub * E + e + 1],
                                    s_rout[:, tsub * D + dch * 512: tsub * D + (dch + 1) * 512],
                                    ALU.mult, ALU.add), inc=sv)
                            stt_hist.append(p.val(sv))
                if dbg:
                    p.wait("sync", sv)
                    p.op("sync", lambda h, hp=hp: h.dma_start(
                        out=rout_scr.ap()[hp * HPT:(hp + 1) * HPT, :].rearrange(
                            "(a p) d -> p a d", p=128),
                        in_=s_rout[:].rearrange("p (a d) -> p a d", a=8)), inc=q_s, k=16)
                p.wait("scalar", sv)
                for tsub in range(8):
                    if tsub > 0:
                        p.wait("sync", q_o)
                        p.wait("scalar", q_o)
                    p.op("sync", lambda h, hp=hp: h.reg_mov(r_a, hp * HPT))
                    p.op("sync", lambda h: h.reg_add(r_a, r_a, r_t))
                    p.op("sync", lambda h, tsub=tsub: h.reg_add(r_a, r_a, tsub * 128))
                    p.op("sync", lambda h: h.reg_mul(r_a, r_a, D))
                    p.op("sync", lambda h: h.dma_start(
                        out=s_x1t[:],
                        in_=bass.AP(x1_scr, r_a, [[D, 128], [1, D]])), inc=q_s, k=16)
                    p.op("scalar", lambda h, tsub=tsub: h.activation(
                        s_outt[:], s_rout[:, tsub * D:(tsub + 1) * D],
                        AF.Square), inc=sa)
                    p.op("vector", lambda h, tsub=tsub: h.reduce_sum(
                        s_stB[:, 0:1], s_rout[:, tsub * D:(tsub + 1) * D],
                        mybir.AxisListType.X))
                    p.wait("vector", sa)
                    vws(lambda h: h.reduce_sum(
                        s_stB[:, 1:2], s_outt[:], mybir.AxisListType.X))
                    vws(lambda h: h.tensor_scalar_mul(
                        s_stB[:, 2:3], s_stB[:, 0:1], 1.0 / D))
                    vws(lambda h: h.tensor_mul(
                        s_stB[:, 3:4], s_stB[:, 2:3], s_stB[:, 2:3]))
                    vws(lambda h: h.tensor_scalar_mul(
                        s_stB[:, 4:5], s_stB[:, 1:2], 1.0 / D))
                    vws(lambda h: h.tensor_sub(
                        s_stB[:, 4:5], s_stB[:, 4:5], s_stB[:, 3:4]))
                    p.wait("scalar", sv)
                    aws_(lambda h: h.activation(
                        s_stB[:, 5:6], s_stB[:, 4:5], AF.Ln, bias=s_c[:, 0:1]))
                    aws_(lambda h: h.activation(
                        s_stB[:, 6:7], s_stB[:, 5:6], AF.Exp, scale=s_c[:, 1:2]))
                    p.wait("vector", sa)
                    p.wait("vector", q_s)
                    p.op("vector", lambda h, tsub=tsub: h.tensor_scalar(
                        s_outt[:], s_rout[:, tsub * D:(tsub + 1) * D],
                        s_stB[:, 2:3], s_stB[:, 6:7], ALU.subtract, ALU.mult))
                    p.op("vector", lambda h: h.tensor_add(
                        s_outt[:], s_outt[:], s_x1t[:]), inc=sv)
                    p.wait("sync", sv)
                    p.op("sync", lambda h, hp=hp, tsub=tsub: h.dma_start(
                        out=out.ap()[hp * HPT + tsub * 128: hp * HPT + (tsub + 1) * 128, :],
                        in_=s_outt[:]), inc=q_o, k=16)
                    if tsub < 7:
                        p.wait("vector", q_o)
                barrier()
            p.wait("sync", q_o)

            if trunc is not None:
                print("truncated to", p.truncate(trunc), "ops")
            ends = p.marks()
            with nc.Block("main") as blk:
                for eng in ENGS:
                    def body(h, eng=eng):
                        p.replay(eng, h, semh, 0, ends[eng])
                    getattr(blk, eng)(body)

    nc._prog = p
    return nc


def _host_layouts(inputs):
    x = np.asarray(inputs["L0_attn_x"], np.float32)
    Wx = np.asarray(inputs["L0_attn_Wx"], np.float32)
    Wdt = np.asarray(inputs["L0_attn_Wdt"], np.float32)
    bdt = np.asarray(inputs["L0_attn_bdt"], np.float32)
    A_log = np.asarray(inputs["L0_attn_A_log"], np.float32)
    Dp = np.asarray(inputs["L0_attn_Dp"], np.float32)
    gate_w = np.asarray(inputs["L0_ffn_gate_w"], np.float32)
    ebias = np.asarray(inputs["L0_ffn_expert_bias"], np.float32)
    w_gate = np.asarray(inputs["L0_ffn_w_gate"], np.float32)
    w_up = np.asarray(inputs["L0_ffn_w_up"], np.float32)
    w_down = np.asarray(inputs["L0_ffn_w_down"], np.float32)
    sw_gate = np.asarray(inputs["L0_ffn_sw_gate"], np.float32)
    sw_up = np.asarray(inputs["L0_ffn_sw_up"], np.float32)
    sw_down = np.asarray(inputs["L0_ffn_sw_down"], np.float32)

    com = {
        "wx2": np.ascontiguousarray(Wx.reshape(DB, 128, 96).transpose(1, 0, 2).reshape(128, DB * 96)),
        "wdt": np.ascontiguousarray(Wdt),
        "bdt2": np.ascontiguousarray(bdt.reshape(DB, 128).T),
        "dp2": np.ascontiguousarray(Dp.reshape(DB, 128).T),
        "alog2": np.ascontiguousarray(A_log.reshape(DB, 128, NS).transpose(1, 0, 2).reshape(128, DB * NS)),
        "gw2": np.ascontiguousarray(gate_w.T.reshape(DB, 128, E).transpose(1, 0, 2).reshape(128, DB * E)),
        "eb128": np.ascontiguousarray(np.broadcast_to(ebias[None, :], (128, E))),
        "ones1": np.ones((1, 128), np.float32),
        "ident": np.eye(128, dtype=np.float32),
        "c128": np.ascontiguousarray(np.broadcast_to(
            np.array([EPS, -0.5, 1.0, 0.0], np.float32)[None, :], (128, 4))),
        "sel16": np.ascontiguousarray(np.kron(np.eye(NS, dtype=np.float32), np.ones((1, 128), np.float32))).astype(ml_dtypes.bfloat16),
        "wgT": np.ascontiguousarray(w_gate.transpose(0, 2, 1)),
        "wuT": np.ascontiguousarray(w_up.transpose(0, 2, 1)),
        "wdT": np.ascontiguousarray(w_down.transpose(0, 2, 1)),
        "swgT": np.ascontiguousarray(sw_gate.T),
        "swuT": np.ascontiguousarray(sw_up.T),
        "swdT": np.ascontiguousarray(sw_down.T),
    }
    in_maps = []
    for c in range(8):
        b, half = c // 2, c % 2
        m = dict(com)
        m["x0"] = np.ascontiguousarray(x[b])
        m["toff"] = np.array([[half * TOK]], np.int32)
        in_maps.append(m)
    return in_maps


def kernel(**inputs):
    in_maps = _host_layouts(inputs)
    nc = build_program(dbg=False)
    res = run_bass_kernel_spmd(nc, in_maps, core_ids=list(range(8)))
    out = np.empty((B, S, D), np.float32)
    for c in range(8):
        b, half = c // 2, c % 2
        out[b, half * TOK:(half + 1) * TOK, :] = res.results[c]["out"]
    return out
